# version 27
# baseline (speedup 1.0000x reference)
import numpy as np
import ml_dtypes
from contextlib import ExitStack

import concourse.bass as bass
import concourse.mybir as mybir
from concourse.bass_utils import run_bass_kernel_spmd
from concourse.alu_op_type import AluOpType

F32 = mybir.dt.float32
BF16 = mybir.dt.bfloat16
I32 = mybir.dt.int32
AF = mybir.ActivationFunctionType

BF = ml_dtypes.bfloat16


def make_cfg(mini=False):
    c = {}
    c["C"] = 8
    c["D"] = 2048
    c["H"] = 16
    c["dh"] = 128
    c["dc"] = 512
    c["dr"] = 32
    c["R"] = 16
    c["eps"] = 1e-5
    if mini:
        c["T"] = 2048
        c["I"] = 1024
    else:
        c["T"] = 8192
        c["I"] = 5504
    c["Tc"] = c["T"] // c["C"]
    c["QTS"] = min(512, c["Tc"])
    c["NQT"] = c["Tc"] // c["QTS"]
    c["KC"] = c["T"] // 128
    c["DT"] = c["D"] // 128
    c["DCT"] = c["dc"] // 128
    c["NLT"] = c["Tc"] // 128
    c["STS"] = min(512, c["Tc"])
    c["NST"] = c["Tc"] // c["STS"]
    c["OT"] = c["I"] // 128
    c["HSL"] = c["D"] // c["C"]
    c["NSLOT"] = (c["OT"] + c["C"] - 1) // c["C"]
    c["WG_T"] = c["OT"] * 2
    c["WD_T"] = c["NSLOT"] * 16
    c["W_TILES"] = 2 * c["WG_T"] + c["WD_T"]
    c["scale"] = float(1.0 / np.sqrt(np.float32(c["dh"] + c["dr"])))
    return c


ENGS = ("T", "V", "S", "G", "Y")
ENG_SEM = {"T": "eT", "V": "eV", "S": "eS", "G": "eG", "Y": None}
ENG_DMA = {"Y": "qS", "S": "qA", "G": "qG"}


class Planner:
    def __init__(self):
        self.thunks = {e: [] for e in ENGS}
        self.cnt = {}
        self.tok = {}
        self.readers = {}
        self.waited = {e: {} for e in ENGS}

    def _wait(self, eng, sem, val):
        if val <= 0:
            return
        if eng == "T" and sem == "eT":
            return
        if self.waited[eng].get(sem, 0) >= val:
            return
        self.waited[eng][sem] = val
        self.thunks[eng].append(("w", sem, val))

    def wait_key(self, eng, key):
        t = self.tok.get(key)
        if t is not None:
            self._wait(eng, t[0], t[1])

    def emit(self, eng, builders, reads=(), writes=(), dma=None, coll=None):
        if callable(builders):
            builders = [builders]
        need = {}
        def _need(t):
            if t is not None and need.get(t[0], 0) < t[1]:
                need[t[0]] = t[1]
        for k in reads:
            _need(self.tok.get(k))
        for k in writes:
            _need(self.tok.get(k))
            for t in self.readers.get(k, ()):
                _need(t)
        for s, v in need.items():
            self._wait(eng, s, v)
        if dma is not None:
            sem, step = dma, 16
        elif coll is not None:
            sem, step = coll, 1
        else:
            sem, step = ENG_SEM[eng], 1
        self.cnt[sem] = self.cnt.get(sem, 0) + step
        tok = (sem, self.cnt[sem])
        for i, b in enumerate(builders):
            last = i == len(builders) - 1
            if not last:
                self.thunks[eng].append(("i", b, None, 0))
            elif coll is not None:
                self.thunks[eng].append(("c", b, sem))
            else:
                self.thunks[eng].append(("i", b, sem, step))
        for k in reads:
            self.readers.setdefault(k, []).append(tok)
        for k in writes:
            self.tok[k] = tok
            self.readers[k] = []
        return tok

    def set_tok(self, key, tok):
        self.tok[key] = tok
        self.readers[key] = []

    def barrier(self, exclude=()):
        snap = dict(self.cnt)
        for e in ENGS:
            for sem, val in snap.items():
                if sem in exclude:
                    continue
                self._wait(e, sem, val)

    def run(self, eng, engobj, sems):
        for th in self.thunks[eng]:
            if th[0] == "w":
                engobj.wait_ge(sems[th[1]], th[2])
            elif th[0] == "i":
                inst = th[1](engobj)
                if th[2] is not None:
                    inst.then_inc(sems[th[2]], th[3])
            else:
                th[1](engobj).then_inc(sems[th[2]])



def build_program(cfg):
    C, D, H, dh, dc, dr, R = (cfg["C"], cfg["D"], cfg["H"], cfg["dh"],
                              cfg["dc"], cfg["dr"], cfg["R"])
    T, Tc, QTS, NQT, KC = cfg["T"], cfg["Tc"], cfg["QTS"], cfg["NQT"], cfg["KC"]
    DT, DCT, NLT, NST, STS = cfg["DT"], cfg["DCT"], cfg["NLT"], cfg["NST"], cfg["STS"]
    OT, HSL, NSLOT = cfg["OT"], cfg["HSL"], cfg["NSLOT"]
    WG_T, WD_T, W_TILES = cfg["WG_T"], cfg["WD_T"], cfg["W_TILES"]
    I_ = cfg["I"]
    eps = cfg["eps"]
    scale = cfg["scale"]
    KROWS = D + dc
    HPG = 128 // (dr // 2)

    nc = bass.Bass("TRN2", target_bir_lowering=False)

    x_in = nc.dram_tensor("x", [Tc, D], F32, kind="ExternalInput")
    w_dkv = nc.dram_tensor("w_dkv", [D, dc], F32, kind="ExternalInput")
    w_uk = nc.dram_tensor("w_uk", [dc, D], F32, kind="ExternalInput")
    w_uv = nc.dram_tensor("w_uv", [dc, D], F32, kind="ExternalInput")
    w_dq = nc.dram_tensor("w_dq", [D, dc], F32, kind="ExternalInput")
    w_uq = nc.dram_tensor("w_uq", [dc, D], F32, kind="ExternalInput")
    w_qr0 = nc.dram_tensor("w_qr0", [dc, dc], F32, kind="ExternalInput")
    w_qr1 = nc.dram_tensor("w_qr1", [dc, dc], F32, kind="ExternalInput")
    w_krg = nc.dram_tensor("w_krg", [D, dc], F32, kind="ExternalInput")
    b_dkv = nc.dram_tensor("b_dkv", [128, DCT], F32, kind="ExternalInput")
    b_uk = nc.dram_tensor("b_uk", [128, DT], F32, kind="ExternalInput")
    b_uv = nc.dram_tensor("b_uv", [128, H], F32, kind="ExternalInput")
    b_dq = nc.dram_tensor("b_dq", [128, DCT], F32, kind="ExternalInput")
    b_uq = nc.dram_tensor("b_uq", [128, DT], F32, kind="ExternalInput")
    b_qr0 = nc.dram_tensor("b_qr0", [2 * dr, H // 2], F32, kind="ExternalInput")
    b_qr1 = nc.dram_tensor("b_qr1", [2 * dr, H // 2], F32, kind="ExternalInput")
    b_krg = nc.dram_tensor("b_krg", [128, DCT], F32, kind="ExternalInput")
    lut_in = nc.dram_tensor("lut", [16], F32, kind="ExternalInput")
    gidx_in = nc.dram_tensor("gidx", [I_, HSL], I32, kind="ExternalInput")
    uidx_in = nc.dram_tensor("uidx", [I_, HSL], I32, kind="ExternalInput")
    didx_in = nc.dram_tensor("didx", [D, NSLOT * 128], I32, kind="ExternalInput")
    cos16_in = nc.dram_tensor("cos16", [128, Tc], F32, kind="ExternalInput")
    sin16_in = nc.dram_tensor("sin16", [128, Tc], F32, kind="ExternalInput")
    sinpm_in = nc.dram_tensor("sinpm", [4 * R, Tc], F32, kind="ExternalInput")
    ident_in = nc.dram_tensor("ident", [128, 128], BF16, kind="ExternalInput")
    y_out = nc.dram_tensor("y", [Tc, D], F32, kind="ExternalOutput")

    gk_in = nc.dram_tensor("gk_in", [KROWS, Tc], BF16)
    gk_out = nc.dram_tensor("gk_out", [C * KROWS, Tc], BF16)
    gv_in = nc.dram_tensor("gv_in", [H * Tc, dh], BF16)
    gv_out = nc.dram_tensor("gv_out", [C * H * Tc, dh], BF16)
    gw_in = nc.dram_tensor("gw_in", [2 * WG_T * 128, 128], BF16)
    gw_out = nc.dram_tensor("gw_out", [C * 2 * WG_T * 128, 128], BF16)
    gd_in = nc.dram_tensor("gd_in", [WD_T * 128, 128], BF16)
    gd_out = nc.dram_tensor("gd_out", [C * WD_T * 128, 128], BF16)
    attn_hbm = nc.dram_tensor("attn_hbm", [Tc, D], BF16)
    rs_hbm = nc.dram_tensor("rs_hbm", [512], F32)

    P = Planner()
    E = P.emit

    with ExitStack() as outer:
        oe = outer.enter_context

        sems = {}
        for name in ("eT", "eV", "eS", "eG",
                     "qST", "qS2", "qX", "qB", "qWB00", "qWB01", "qWB10", "qWB11", "qGI",
                     "qSK0", "qSK1", "qKR", "qAT",
                     *[f"qK{i}" for i in range(8)],
                     *[f"qR{i}" for i in range(8)],
                     *[f"qV{i}" for i in range(8)],
                     "qAW0", "qAW1", "qFG0", "qFG1", "qFD0", "qFD1",
                     "qCX", "qCA", "qY",
                     "ccK", "ccV", "ccW", "ccD"):
            sems[name] = oe(nc.semaphore(name))

        sxnT = oe(nc.sbuf_tensor("sxnT", [128, DT * Tc], BF16))
        sident = oe(nc.sbuf_tensor("sident", [128, 128], BF16))
        sones = oe(nc.sbuf_tensor("sones", [128, 1], F32))
        sonesb = oe(nc.sbuf_tensor("sonesb", [128, 1], BF16))
        sonesr = oe(nc.sbuf_tensor("sonesr", [1, 128], F32))
        slut = oe(nc.sbuf_tensor("slut", [1, 16], F32))
        slutbc = oe(nc.sbuf_tensor("slutbc", [128, 16], F32))

        psS = [oe(nc.psum_tensor(f"psS{i}", [128, 1024], F32)) for i in range(2)]
        psAV = oe(nc.psum_tensor("psAV", [128, 512], F32))
        psR = oe(nc.psum_tensor("psR", [128, 512], F32))
        pt = [oe(nc.psum_tensor(f"pt{i}", [128, 1024], BF16)) for i in range(2)]
        ps = [psS[0][:, 0:512], psS[1][:, 0:512], psAV[:, :], psR[:, :]]

        def dma(eng, q, dst, src, reads=(), writes=()):
            return E(eng, lambda e, d=dst, s=src: e.dma_start(out=d, in_=s),
                     reads=reads, writes=writes, dma=q)

        dma("Y", "qST", sident[:, :], ident_in[:, :], writes=[("ident",)])
        dma("Y", "qST", slut[0:1, :], lut_in[None, :], writes=[("slut",)])
        for _k in (("ident",), ("slut",)):
            P.tok[_k] = ("qST", P.cnt["qST"])
        E("V", lambda e: e.memset(sones[:, :], 1.0), writes=[("ones",)])
        E("V", lambda e: e.memset(sonesb[:, :], 1.0), writes=[("onesb",)])
        E("V", lambda e: e.memset(sonesr[:, :], 1.0), writes=[("onesr",)])
        E("T", lambda e: e.matmul(ps[2][:, 0:16], sonesr[0:1, :], slut[0:1, :],
                                  start=True, stop=True),
          reads=[("onesr",), ("slut",)], writes=[("ps2",)])
        E("V", lambda e: e.tensor_copy(slutbc[:, :], ps[2][:, 0:16]),
          reads=[("ps2",)], writes=[("lutbc",)])

        xnT_keys = [("xnT", cc) for cc in range(DT)]

        with ExitStack() as phAB:
            abe = phAB.enter_context
            sQcT = abe(nc.sbuf_tensor("sQcT", [128, H * Tc], BF16))
            sQrT = abe(nc.sbuf_tensor("sQrT", [128, H * Tc], BF16))
            MJF = 1024
            smidx = abe(nc.sbuf_tensor("smidx", [128, MJF], I32))
            smbf = [abe(nc.sbuf_tensor(f"smbf{i}", [128, MJF], BF16)) for i in range(2)]
            smacc = [abe(nc.sbuf_tensor(f"smacc{i}", [128, MJF], BF16)) for i in range(2)]
            smtmp = abe(nc.sbuf_tensor("smtmp", [128, MJF], BF16))
            swst = [abe(nc.sbuf_tensor(f"swst{i}", [128, MJF], BF16)) for i in range(2)]


            with ExitStack() as phA:
                ae = phA.enter_context
                scos = ae(nc.sbuf_tensor("scos", [128, Tc], BF16))
                ssin = ae(nc.sbuf_tensor("ssin", [128, Tc], BF16))
                ssinpm = ae(nc.sbuf_tensor("ssinpm", [4 * R, Tc], BF16))
                swstage = [ae(nc.sbuf_tensor(f"swstage{i}", [128, 1024], F32))
                           for i in range(2)]
                s_x = ae(nc.sbuf_tensor("s_x", [128, D], F32))
                s_xn = ae(nc.sbuf_tensor("s_xn", [128, D], BF16))
                s_st = ae(nc.sbuf_tensor("s_st", [128, 32], F32))
                s_mv = ae(nc.sbuf_tensor("s_mv", [128, 4], F32))
                sckvT = ae(nc.sbuf_tensor("sckvT", [128, DCT * Tc], BF16))
                scqT = ae(nc.sbuf_tensor("scqT", [128, DCT * Tc], BF16))
                swbufs = [ae(nc.sbuf_tensor(f"swbuf{i}", [128, DT * 512], BF16)) for i in range(2)]
                sbias = ae(nc.sbuf_tensor("sbias", [128, 32], F32))
                sbq = ae(nc.sbuf_tensor("sbq", [128, DT], F32))
                sbqr = ae(nc.sbuf_tensor("sbqr", [2 * dr, H], F32))
                skst = [ae(nc.sbuf_tensor(f"skst{i}", [128, 512], BF16)) for i in range(2)]
                skrp = [ae(nc.sbuf_tensor(f"skrp{i}", [128, Tc], BF16)) for i in range(2)]
                sk1 = ae(nc.sbuf_tensor("sk1", [128, Tc], F32))
                sk2 = ae(nc.sbuf_tensor("sk2", [128, Tc], F32))
                sqrp = [ae(nc.sbuf_tensor(f"sqrp{i}", [4 * R, Tc], BF16)) for i in range(2)]

                dma("G", "qS2", scos[:, :], cos16_in[:, :], writes=[("cos",)])
                dma("G", "qS2", ssin[:, :], sin16_in[:, :], writes=[("sin",)])
                dma("G", "qS2", ssinpm[:, :], sinpm_in[:, :], writes=[("sinpm",)])
                for _k in (("cos",), ("sin",), ("sinpm",)):
                    P.tok[_k] = ("qS2", P.cnt["qS2"])
                for i in range(NLT):
                    dma("Y", "qX", s_x[:, :], x_in[i * 128:(i + 1) * 128, :],
                        reads=[("x",)], writes=[("x",)])
                    nbn = D // 512
                    for bnc in range(nbn):
                        E("V", lambda e, q=bnc: e.bn_stats(
                            s_st[:, q * 6:q * 6 + 6],
                            s_x[:, q * 512:(q + 1) * 512]),
                          reads=[("x",), ("st",)], writes=[("st",)])
                    E("V", lambda e, nn=nbn: e.bn_aggr(s_mv[:, 0:2], s_st[:, 0:6 * nn]),
                      reads=[("st",), ("mv",)], writes=[("mv",)])
                    E("V", lambda e: e.tensor_scalar(out=s_mv[:, 2:3], in0=s_mv[:, 1:2],
                                                     scalar1=float(eps), scalar2=None,
                                                     op0=AluOpType.add),
                      reads=[("mv",)], writes=[("veps",)])
                    E("S", lambda e: e.activation(out=s_st[:, 30:31], in_=s_mv[:, 2:3],
                                                  func=AF.Sqrt),
                      reads=[("veps",), ("sqv",)], writes=[("sqv",)])
                    E("V", lambda e: e.reciprocal(s_mv[:, 2:3], s_st[:, 30:31]),
                      reads=[("sqv",), ("veps",)], writes=[("rstd",)])
                    E("V", lambda e: e.tensor_tensor(out=s_mv[:, 3:4], in0=s_mv[:, 0:1],
                                                     in1=s_mv[:, 2:3], op=AluOpType.mult),
                      reads=[("rstd",), ("mv",), ("nmrs",)], writes=[("mrs",)])
                    E("V", lambda e: e.tensor_scalar(out=s_mv[:, 3:4], in0=s_mv[:, 3:4],
                                                     scalar1=-1.0, scalar2=None,
                                                     op0=AluOpType.mult),
                      reads=[("mrs",)], writes=[("nmrs",)])
                    E("S", lambda e: e.activation(out=s_xn[:, :], in_=s_x[:, :],
                                                  func=AF.Identity,
                                                  bias=s_mv[:, 3:4],
                                                  scale=s_mv[:, 2:3]),
                      reads=[("x",), ("rstd",), ("nmrs",), ("xn",)], writes=[("xn",)])
                    for c in range(DT):
                        tp = c % 2
                        E("T", lambda e, cc=c, t=tp: e.transpose(
                            pt[t][:, 0:128], s_xn[:, cc * 128:(cc + 1) * 128],
                            sident[:, :]),
                          reads=[("xn",), ("ident",)], writes=[(f"pt{tp}",)])
                        E("S", lambda e, cc=c, ii=i, t=tp: e.copy(
                            sxnT[:, cc * Tc + ii * 128: cc * Tc + (ii + 1) * 128],
                            pt[t][:, 0:128]),
                          reads=[(f"pt{tp}",)], writes=[("xnT", c)])

                dma("Y", "qB", sbias[:, 0:DCT], b_dkv[:, :], writes=[("bias",)])
                dma("Y", "qB", sbias[:, 4:4 + DCT], b_dq[:, :], writes=[("bias",)])
                dma("Y", "qB", sbias[:, 8:8 + DCT], b_krg[:, :], writes=[("bias",)])
                dma("Y", "qB", sbias[:, 12:12 + DT], b_uk[:, :], writes=[("bias",)])
                dma("Y", "qB", sbq[:, :], b_uq[:, :], writes=[("biasq",)])
                dma("Y", "qB", sbqr[:, 0:H // 2], b_qr0[:, :], writes=[("biasqr",)])
                dma("Y", "qB", sbqr[:, H // 2:H], b_qr1[:, :], writes=[("biasqr",)])

                for _k in (("bias",), ("biasq",), ("biasqr",)):
                    P.tok[_k] = ("qB", P.cnt["qB"])

                ckv_keys = [("ckvT", cc) for cc in range(DCT)]

                def wload(wsrc, slot, ncols):
                    wb = swbufs[slot]
                    wkey = ("swbuf", slot)
                    nch = wsrc.shape[0] // 128
                    m = wsrc.shape[1]
                    src3 = wsrc.rearrange("(c p) m -> p c m", p=128)
                    pieces_per_chunk = max(1, m // 1024)
                    pw = m // pieces_per_chunk
                    first = True
                    for ccx in range(nch):
                        for pz in range(pieces_per_chunk):
                            par = (ccx * pieces_per_chunk + pz) % 2
                            dma("Y", f"qWB{slot}{par}",
                                swstage[par][:, 0:pw],
                                src3[:, ccx, pz * pw:(pz + 1) * pw],
                                reads=[("wstg", par)], writes=[("wstg", par)])
                            E("V", lambda e, p=par, c=ccx, z=pz, w=wb, ww=pw:
                              e.tensor_copy(
                                  w[:, c * m + z * ww: c * m + (z + 1) * ww],
                                  swstage[p][:, 0:ww]),
                              reads=[("wstg", par)] + ([wkey] if first else []),
                              writes=[wkey])
                            first = False
                cq_keys = [("cqT", cc) for cc in range(DCT)]

                def down_proj(wsrc, out_sb, out_keyname, biascol0, bias_sb, slot):
                    wb = swbufs[slot]
                    wkey = ("swbuf", slot)
                    wload(wsrc, slot, 512)
                    for mt in range(DCT):
                        for st in range(NST):
                            bk = (mt * NST + st) % 2
                            E("T", [
                                (lambda e, m=mt, s=st, cc=ccc, fi=(ccc == 0),
                                 la=(ccc == DT - 1), b=bk:
                                 e.matmul(ps[b][:, 0:STS],
                                          wb[:, cc * 512 + m * 128:
                                             cc * 512 + m * 128 + 128],
                                          sxnT[:, cc * Tc + s * STS:
                                               cc * Tc + s * STS + STS],
                                          start=fi, stop=la))
                                for ccc in range(DT)],
                              reads=[wkey] + xnT_keys, writes=[(f"ps{bk}",)])
                            if out_keyname == "krp":
                                E("S", lambda e, m=mt, s=st, b=bk: e.activation(
                                    out=skrp[m % 2][:, s * STS: s * STS + STS],
                                    in_=ps[b][:, 0:STS], func=AF.Identity,
                                    bias=bias_sb[:, biascol0 + m:biascol0 + m + 1],
                                    scale=1.0),
                                  reads=[(f"ps{bk}",), ("bias",), ("krp", mt % 2)],
                                  writes=[("krp", mt)])
                            else:
                                E("S", lambda e, m=mt, s=st, b=bk: e.activation(
                                    out=out_sb[:, m * Tc + s * STS:
                                               m * Tc + s * STS + STS],
                                    in_=ps[b][:, 0:STS], func=AF.Identity,
                                    bias=bias_sb[:, biascol0 + m:biascol0 + m + 1],
                                    scale=1.0),
                                  reads=[(f"ps{bk}",), ("bias",)],
                                  writes=[(out_keyname, mt)])

                down_proj(w_dkv, sckvT, "ckvT", 0, sbias, 0)
                down_proj(w_dq, scqT, "cqT", 4, sbias, 1)

                wb, wkey = swbufs[0], ("swbuf", 0)
                wload(w_krg, 0, 512)
                for g in range(DCT // 2):
                    for loc in range(2):
                        gt = 2 * g + loc
                        for st in range(NST):
                            bk = (gt * NST + st) % 2
                            E("T", [
                                (lambda e, m=gt, s=st, cc=ccc, fi=(ccc == 0),
                                 la=(ccc == DT - 1), b=bk, w=wb:
                                 e.matmul(ps[b][:, 0:STS],
                                          w[:, cc * 512 + m * 128:
                                            cc * 512 + m * 128 + 128],
                                          sxnT[:, cc * Tc + s * STS:
                                               cc * Tc + s * STS + STS],
                                          start=fi, stop=la))
                                for ccc in range(DT)],
                              reads=[wkey] + xnT_keys, writes=[(f"ps{bk}",)])
                            E("S", lambda e, m=gt, s=st, b=bk, l=loc: e.activation(
                                out=skrp[l][:, s * STS: s * STS + STS],
                                in_=ps[b][:, 0:STS], func=AF.Identity,
                                bias=sbias[:, 8 + m:9 + m], scale=1.0),
                              reads=[(f"ps{bk}",), ("bias",), ("krp", loc)],
                              writes=[("krp", loc)])
                    x1k, x2k = ("krp", 0), ("krp", 1)
                    E("V", lambda e: e.tensor_tensor(out=sk1[:, :], in0=skrp[0][:, :],
                                                     in1=scos[:, :], op=AluOpType.mult),
                      reads=[x1k, ("cos",), ("sk1",)], writes=[("sk1",)])
                    E("V", lambda e: e.tensor_tensor(out=sk2[:, :], in0=skrp[1][:, :],
                                                     in1=ssin[:, :], op=AluOpType.mult),
                      reads=[x2k, ("sin",), ("sk2",)], writes=[("sk2",)])
                    E("V", lambda e: e.tensor_tensor(out=smtmp[:, 0:Tc], in0=sk1[:, :],
                                                     in1=sk2[:, :], op=AluOpType.subtract),
                      reads=[("sk1",), ("sk2",), ("mtmp",)], writes=[("mtmp",)])
                    dma("Y", "qKR", gk_in[D + 2 * g * 128: D + 2 * g * 128 + 128, :],
                        smtmp[:, 0:Tc], reads=[("mtmp",)], writes=[("gk_rope", 2 * g)])
                    E("V", lambda e: e.tensor_tensor(out=sk1[:, :], in0=skrp[1][:, :],
                                                     in1=scos[:, :], op=AluOpType.mult),
                      reads=[x2k, ("cos",), ("sk1",)], writes=[("sk1",)])
                    E("V", lambda e: e.tensor_tensor(out=sk2[:, :], in0=skrp[0][:, :],
                                                     in1=ssin[:, :], op=AluOpType.mult),
                      reads=[x1k, ("sin",), ("sk2",)], writes=[("sk2",)])
                    E("V", lambda e: e.tensor_tensor(out=smtmp[:, 0:Tc], in0=sk1[:, :],
                                                     in1=sk2[:, :], op=AluOpType.add),
                      reads=[("sk1",), ("sk2",), ("mtmp",)], writes=[("mtmp",)])
                    dma("Y", "qKR", gk_in[D + (2 * g + 1) * 128: D + (2 * g + 1) * 128 + 128, :],
                        smtmp[:, 0:Tc], reads=[("mtmp",)], writes=[("gk_rope", 2 * g + 1)])

                wb, wkey = swbufs[1], ("swbuf", 1)
                wload(w_uk, 1, D)
                for dt in range(DT):
                    for st in range(NST):
                        bk = (dt * NST + st) % 2
                        sp = bk
                        E("T", [
                            (lambda e, m=dt, s=st, cc=ccc, fi=(ccc == 0),
                             la=(ccc == DCT - 1), b=bk, w=wb:
                             e.matmul(ps[b][:, 0:STS],
                                      w[:, cc * D + m * 128: cc * D + m * 128 + 128],
                                      sckvT[:, cc * Tc + s * STS: cc * Tc + s * STS + STS],
                                      start=fi, stop=la))
                            for ccc in range(DCT)],
                          reads=[wkey] + ckv_keys, writes=[(f"ps{bk}",)])
                        E("S", lambda e, m=dt, b=bk, p=sp: e.activation(
                            out=skst[p][:, 0:STS], in_=ps[b][:, 0:STS],
                            func=AF.Identity, bias=sbias[:, 12 + m:13 + m], scale=1.0),
                          reads=[(f"ps{bk}",), ("bias",), ("kst", sp)],
                          writes=[("kst", sp)])
                        dma("Y", f"qSK{sp}",
                            gk_in[dt * 128:(dt + 1) * 128, st * STS:st * STS + STS],
                            skst[sp][:, 0:STS], reads=[("kst", sp)],
                            writes=[("gk_c", dt, st)])

                wb, wkey = swbufs[0], ("swbuf", 0)
                wload(w_uv, 0, D)
                for dd in range(D // 512):
                    for st2 in range(NLT):
                        bk = (st2 * (D // 512) + dd) % 2
                        sp = bk
                        blds = [
                            (lambda e, s=st2, d=dd, cc=ccc, fi=(ccc == 0),
                             la=(ccc == DCT - 1), b=bk, w=wb:
                             e.matmul(ps[b][:, 0:512],
                                      sckvT[:, cc * Tc + s * 128: cc * Tc + s * 128 + 128],
                                      w[:, cc * D + d * 512: cc * D + (d + 1) * 512],
                                      start=fi, stop=la))
                            for ccc in range(DCT)]
                        E("T", blds,
                          reads=[wkey] + ckv_keys,
                          writes=[(f"ps{bk}",)])
                        E("S", lambda e, b=bk, p=sp: e.copy(
                            skst[p][:, 0:512], ps[b][:, 0:512]),
                          reads=[(f"ps{bk}",), ("kst", sp)], writes=[("kst", sp)])
                        for hh in range(4):
                            h = dd * 4 + hh
                            dma("Y", f"qSK{sp}",
                                gv_in[h * Tc + st2 * 128: h * Tc + (st2 + 1) * 128, :],
                                skst[sp][:, hh * 128:(hh + 1) * 128],
                                reads=[("kst", sp)], writes=[("gv", h, st2)])

                kreads = ([("gk_c", dt, st) for dt in range(DT) for st in range(NST)] +
                          [("gk_rope", i) for i in range(DCT)])
                E("G", lambda e: e.collective_compute(
                    "AllGather", AluOpType.bypass,
                    replica_groups=[list(range(C))],
                    ins=[gk_in.ap().opt()], outs=[gk_out.ap().opt()]),
                  reads=kreads, writes=[("gk_out",)], coll="ccK")
                vreads = [("gv", h, s2) for h in range(H) for s2 in range(NLT)]
                E("G", lambda e: e.collective_compute(
                    "AllGather", AluOpType.bypass,
                    replica_groups=[list(range(C))],
                    ins=[gv_in.ap().opt()], outs=[gv_out.ap().opt()]),
                  reads=vreads, writes=[("gv_out",)], coll="ccV")

                wb, wkey = swbufs[1], ("swbuf", 1)
                wload(w_uq, 1, D)
                for dt in range(DT):
                    for st in range(NST):
                        bk = (dt * NST + st) % 2
                        E("T", [
                            (lambda e, m=dt, s=st, cc=ccc, fi=(ccc == 0),
                             la=(ccc == DCT - 1), b=bk, w=wb:
                             e.matmul(ps[b][:, 0:STS],
                                      w[:, cc * D + m * 128: cc * D + m * 128 + 128],
                                      scqT[:, cc * Tc + s * STS: cc * Tc + s * STS + STS],
                                      start=fi, stop=la))
                            for ccc in range(DCT)],
                          reads=[wkey] + cq_keys, writes=[(f"ps{bk}",)])
                        E("S", lambda e, m=dt, s=st, b=bk: e.activation(
                            out=sQcT[:, m * Tc + s * STS: m * Tc + s * STS + STS],
                            in_=ps[b][:, 0:STS], func=AF.Identity,
                            bias=sbq[:, m:m + 1], scale=1.0),
                          reads=[(f"ps{bk}",), ("biasq",)], writes=[("QcT",)])

                E("V", lambda e: e.memset(sQrT[:, :], 0.0), writes=[("QrT",)])
                wb, wkey = swbufs[0], ("swbuf", 0)
                wload(w_qr0, 0, dc)
                src3b = w_qr1.rearrange("(c p) m -> p c m", p=128)
                for ccx in range(DCT):
                    par = ccx % 2
                    dma("Y", f"qWB0{par}", swstage[par][:, 0:dc],
                        src3b[:, ccx, :],
                        reads=[("wstg", par)], writes=[("wstg", par)])
                    E("V", lambda e, p=par, c=ccx: e.tensor_copy(
                        wb[:, DCT * dc + c * dc: DCT * dc + (c + 1) * dc],
                        swstage[p][:, 0:dc]),
                      reads=[("wstg", par), wkey], writes=[wkey])
                for hp in range(H // 2):
                    h2 = 2 * hp
                    for v in range(2):
                        off = (0 if v == 0 else DCT * dc)
                        pp = ps[2] if v == 0 else ps[3]
                        pk = "ps2" if v == 0 else "ps3"
                        for st in range(NST):
                            E("T", [
                                (lambda e, s=st, cc=ccc, fi=(ccc == 0),
                                 la=(ccc == DCT - 1), o=off, q=pp, hh2=h2, w=wb:
                                 e.matmul(q[0:2 * dr, 0:STS],
                                          w[:, o + cc * dc + hh2 * dr:
                                            o + cc * dc + (hh2 + 2) * dr],
                                          scqT[:, cc * Tc + s * STS:
                                               cc * Tc + s * STS + STS],
                                          start=fi, stop=la))
                                for ccc in range(DCT)],
                              reads=[wkey] + cq_keys,
                              writes=[(pk,)])
                            E("S", lambda e, s=st, vv=v, q=pp, hpp=hp: e.activation(
                                out=sqrp[vv][:, s * STS: s * STS + STS],
                                in_=q[0:2 * dr, 0:STS], func=AF.Identity,
                                bias=sbqr[:, vv * (H // 2) + hpp:
                                          vv * (H // 2) + hpp + 1], scale=1.0),
                              reads=[(pk,), ("biasqr",), ("qrp", v)],
                              writes=[("qrp", v)])
                    E("V", lambda e: e.tensor_tensor(
                        out=sk1[0:4 * R, :], in0=sqrp[0][:, :], in1=scos[0:4 * R, :],
                        op=AluOpType.mult),
                      reads=[("qrp", 0), ("cos",), ("sk1",)], writes=[("sk1",)])
                    E("V", lambda e: e.tensor_tensor(
                        out=sk2[0:4 * R, :], in0=sqrp[1][:, :], in1=ssinpm[:, :],
                        op=AluOpType.mult),
                      reads=[("qrp", 1), ("sinpm",), ("sk2",)], writes=[("sk2",)])
                    for dh2 in range(2):
                        E("V", lambda e, hh=h2 + dh2, o=dh2 * 2 * R: e.tensor_tensor(
                            out=sQrT[0:2 * R, hh * Tc:(hh + 1) * Tc],
                            in0=sk1[o:o + 2 * R, :] if o == 0 else sk1[2 * R:4 * R, :],
                            in1=sk2[o:o + 2 * R, :] if o == 0 else sk2[2 * R:4 * R, :],
                            op=AluOpType.add),
                          reads=[("sk1",), ("sk2",)], writes=[("QrT",)])

            mat_jobs = []
            OTJ = MJF // HSL
            for mat_i, src in ((0, gidx_in), (1, uidx_in)):
                ot0 = 0
                while ot0 < OT:
                    n = min(OTJ, OT - ot0)
                    mat_jobs.append(("gu", mat_i, src, ot0, n))
                    ot0 += n
            n_gu_jobs = len(mat_jobs)
            DTJ = MJF // 128
            for slot in range(NSLOT):
                for half in range((DT + DTJ - 1) // DTJ):
                    mat_jobs.append(("dn", 2, didx_in, (slot, half), DTJ))
            njobs = len(mat_jobs)

            mat_queue = []

            def plan_mat_job(jid):
                kind, mi, src_t, a0, n = mat_jobs[jid]
                par = jid % 2
                steps = []
                if kind == "gu":
                    fd = n * HSL
                    steps.append(lambda f=fd, a=a0, nn=n, s=src_t: dma(
                        "G", "qGI", smidx[:, 0:f].rearrange("p (o h) -> p o h", h=HSL),
                        s.rearrange("(o p) h -> p o h", p=128)[:, a:a + nn, :],
                        reads=[("midx",)], writes=[("midx",)]))
                else:
                    slot, half = a0
                    fd = n * 128
                    steps.append(lambda f=fd, sl=slot, hf=half, s=src_t: dma(
                        "G", "qGI", smidx[:, 0:f].rearrange("p (t o) -> p t o", o=128),
                        s.rearrange("(t p) o -> p t o", p=128)
                        [:, hf * DTJ:(hf + 1) * DTJ, sl * 128:(sl + 1) * 128],
                        reads=[("midx",)], writes=[("midx",)]))
                steps.append(lambda p=par, f=fd: E(
                    "V", lambda e: e.tensor_copy(smbf[p][:, 0:f], smidx[:, 0:f]),
                    reads=[("midx",), ("mbf", p)], writes=[("mbf", p)]))
                for j in range(16):
                    def sel_step(p=par, f=fd, jj=j):
                        E("V", lambda e: e.tensor_scalar(
                            out=smtmp[:, 0:f], in0=smbf[p][:, 0:f],
                            scalar1=float(jj), scalar2=slutbc[:, jj:jj + 1],
                            op0=AluOpType.is_equal, op1=AluOpType.mult),
                          reads=[("mbf", p), ("lutbc",), ("mtmp",)],
                          writes=[("mtmp",)])
                        if jj == 0:
                            E("V", lambda e: e.tensor_copy(
                                smacc[p][:, 0:f], smtmp[:, 0:f]),
                              reads=[("mtmp",), ("macc", p)], writes=[("macc", p)])
                        else:
                            E("V", lambda e: e.tensor_tensor(
                                out=smacc[p][:, 0:f], in0=smacc[p][:, 0:f],
                                in1=smtmp[:, 0:f], op=AluOpType.add),
                              reads=[("mtmp",), ("macc", p)], writes=[("macc", p)])
                    steps.append(sel_step)
                for _sp in range(6):
                    steps.append(lambda: None)
                nblk = fd // 128
                def tr_step(p=par, nb2=nblk):
                    for bb in range(nb2):
                        tp = bb % 2
                        E("T", lambda e, b=bb, t=tp: e.transpose(
                            pt[t][:, 0:128], smacc[p][:, b * 128:(b + 1) * 128],
                            sident[:, :]),
                          reads=[("macc", p), ("ident",)], writes=[(f"pt{tp}",)])
                        E("S", lambda e, b=bb, t=tp: e.copy(
                            swst[p][:, b * 128:(b + 1) * 128], pt[t][:, 0:128]),
                          reads=[(f"pt{tp}",), ("wst", p)], writes=[("wst", p)])
                steps.append(tr_step)
                if kind == "gu":
                    nb = HSL // 128
                    def st_step(p=par, m=mi, a=a0, nn=n, jj=jid):
                        for j2 in range(nn):
                            t0 = (m * WG_T + (a + j2) * nb) * 128
                            dma("S", f"qAW{p}", gw_in[t0:t0 + nb * 128, :]
                                .rearrange("(j p) q -> p j q", p=128),
                                swst[p][:, j2 * HSL: j2 * HSL + HSL]
                                .rearrange("p (j q) -> p j q", q=128),
                                reads=[("wst", p)], writes=[("gw", jj, j2)])
                    steps.append(st_step)
                else:
                    slot, half = a0
                    def st_step(p=par, sl=slot, hf=half, nn=n, jj=jid):
                        dview = gd_in[0:DT * NSLOT * 128, :]\
                            .rearrange("(d s p) q -> p d s q", p=128, s=NSLOT)
                        dma("S", f"qAW{p}",
                            dview[:, hf * DTJ:(hf + 1) * DTJ, sl, :],
                            swst[p][:, 0:nn * 128]
                            .rearrange("p (t q) -> p t q", q=128),
                            reads=[("wst", p)], writes=[("gw", jj, 0)])
                    steps.append(st_step)
                mat_queue.extend(steps)

            for _j in range(min(4, njobs)):
                plan_mat_job(_j)
            mat_next = [min(4, njobs)]

            P.barrier()

            with ExitStack() as phB:
                be = phB.enter_context
                sKc = be(nc.sbuf_tensor("sKc", [128, T], BF16))
                sKr = be(nc.sbuf_tensor("sKr", [128, T], BF16))
                sV = be(nc.sbuf_tensor("sV", [128, T], BF16))
                sP = [be(nc.sbuf_tensor(f"sP{i}", [128, 1024], BF16)) for i in range(6)]
                sracc_v = be(nc.sbuf_tensor("sracc_v", [128, 1024], F32))
                sracc_g = be(nc.sbuf_tensor("sracc_g", [128, 1024], F32))
                sracc2 = be(nc.sbuf_tensor("sracc2", [128, 1024], F32))
                sracc = be(nc.sbuf_tensor("sracc", [128, 512], F32))
                srow = be(nc.sbuf_tensor("srow", [1, 512], F32))
                sbuv = be(nc.sbuf_tensor("sbuv", [128, H], F32))
                srbc = be(nc.sbuf_tensor("srbc", [128, 512], F32))
                soT2 = be(nc.sbuf_tensor("soT2", [128, 512], BF16))
                soT = be(nc.sbuf_tensor("soT", [128, 512], BF16))
                sanst = be(nc.sbuf_tensor("sanst", [128, 512], BF16))

                gkr = gk_out.rearrange("(r k) s -> k r s", k=KROWS)
                dma("Y", "qB", sbuv[:, :], b_uv[:, :], writes=[("buv",)])
                P.tok[("buv",)] = ("qB", P.cnt["qB"])
                E("V", lambda e: e.memset(sKr[:, :], 0.0), writes=[("Kr", r) for r in range(C)])
                pending_fin = []

                for h in range(H):
                    g, hl = h // HPG, h % HPG
                    b1 = D + 2 * g * 128 + hl * (dr // 2)
                    b2 = b1 + 128
                    for r in range(C):
                        dma("Y", f"qK{r}", sKc[:, r * Tc:(r + 1) * Tc],
                            gkr[h * 128:(h + 1) * 128, r, :],
                            reads=[("gk_out",), ("Kc", r)], writes=[("Kc", r)])
                    for r in range(C):
                        dma("Y", f"qV{r}",
                            sV[:, r * NLT * dh:(r + 1) * NLT * dh]
                            .rearrange("p (s2 d) -> p s2 d", d=dh),
                            gv_out[r * H * Tc + h * Tc:
                                   r * H * Tc + (h + 1) * Tc, :]
                            .rearrange("(s2 p) d -> p s2 d", p=128),
                            reads=[("gv_out",), ("V", r)], writes=[("V", r)])
                    for r in range(C):
                        dma("Y", f"qR{r}", sKr[0:dr // 2, r * Tc:(r + 1) * Tc],
                            gkr[b1:b1 + dr // 2, r, :],
                            reads=[("gk_out",), ("Kr", r)], writes=[("Kr", r)])
                        dma("Y", f"qR{r}", sKr[dr // 2:dr, r * Tc:(r + 1) * Tc],
                            gkr[b2:b2 + dr // 2, r, :],
                            reads=[("gk_out",)], writes=[("Kr", r)])
                    for qt in range(NQT):
                        q0 = qt * QTS
                        abk = "ps2"
                        NSC = KC // 2
                        assert NSC >= 5

                        def s_chain(ck, hh=h, q=q0):
                            sb = (ck // 2) % 2
                            lo = (ck % 2) * QTS
                            E("T", [
                                (lambda e, c=ck, b=sb, l=lo:
                                 e.matmul(psS[b][:, l:l + QTS],
                                          sKc[:, c * 128:(c + 1) * 128],
                                          sQcT[:, hh * Tc + q: hh * Tc + q + QTS],
                                          start=True, stop=False)),
                                (lambda e, c=ck, b=sb, l=lo:
                                 e.matmul(psS[b][:, l:l + QTS],
                                          sKr[:, c * 128:(c + 1) * 128],
                                          sQrT[:, hh * Tc + q: hh * Tc + q + QTS],
                                          start=False, stop=True)),
                            ], reads=[("Kc", ck // (KC // C)),
                                      ("Kr", ck // (KC // C)),
                                      ("QcT",), ("QrT",)],
                               writes=[(f"psS{sb}",)])

                        def do_av(sc):
                            pb = sc % 6
                            for half in range(2):
                                ck = 2 * sc + half
                                av_writes = [("ps2",)] if ck in (0, KC - 1) else []
                                E("T", lambda e, c=ck, p=pb, l=half * QTS,
                                  fi=(ck == 0), la=(ck == KC - 1):
                                  e.matmul(psAV[:, 0:QTS],
                                           sV[:, c * 128:(c + 1) * 128],
                                           sP[p][:, l:l + QTS], start=fi, stop=la),
                                  reads=[("P", pb), ("V", ck // (KC // C))],
                                  writes=av_writes)

                        s_chain(0)
                        s_chain(1)
                        vfirst, gfirst = [True], [True]
                        for sc in range(NSC):
                            sb = sc % 2
                            pb = sc % 6
                            E("S", lambda e, b=sb, p=pb: e.activation(
                                out=sP[p][:, 0:2 * QTS], in_=psS[b][:, 0:2 * QTS],
                                func=AF.Exp, scale=scale),
                              reads=[(f"psS{sb}",), ("P", pb)], writes=[("P", pb)])
                            if sc + 1 < NSC:
                                s_chain(2 * sc + 2)
                                s_chain(2 * sc + 3)
                            if sc >= 1:
                                fin_len = 8 + QTS // 128
                                npop = -(-fin_len // (NSC - 1))
                                for _ in range(npop):
                                    if pending_fin:
                                        pending_fin.pop(0)()
                            if sc >= 1:
                                do_av(sc - 1)
                            if sc == NSC - 1:
                                do_av(sc)
                            sel = ("V", "V", "G", "G", "V")[sc % 5]
                            eng = "V" if sel == "V" else "G"
                            tgt = ("racc_v",) if sel == "V" else ("racc_g",)
                            dstb = sracc_v if sel == "V" else sracc_g
                            fref = vfirst if sel == "V" else gfirst
                            if fref[0]:
                                fref[0] = False
                                E(eng, lambda e, p=pb, d=dstb: e.tensor_copy(
                                    d[:, 0:2 * QTS], sP[p][:, 0:2 * QTS]),
                                  reads=[("P", pb), tgt], writes=[tgt])
                            else:
                                E(eng, lambda e, p=pb, d=dstb: e.tensor_tensor(
                                    out=d[:, 0:2 * QTS], in0=d[:, 0:2 * QTS],
                                    in1=sP[p][:, 0:2 * QTS], op=AluOpType.add),
                                  reads=[("P", pb), tgt], writes=[tgt])
                            for _mq in range(2):
                                if mat_queue:
                                    mat_queue.pop(0)()
                        E("V", lambda e: e.tensor_tensor(
                            out=sracc2[:, 0:2 * QTS], in0=sracc_v[:, 0:2 * QTS],
                            in1=sracc_g[:, 0:2 * QTS], op=AluOpType.add),
                          reads=[("racc_v",), ("racc_g",), ("racc2",)],
                          writes=[("racc2",)])
                        E("V", lambda e: e.tensor_tensor(
                            out=sracc[:, 0:QTS], in0=sracc2[:, 0:QTS],
                            in1=sracc2[:, QTS:2 * QTS], op=AluOpType.add),
                          reads=[("racc2",), ("racc",)], writes=[("racc",)])
                        E("S", lambda e: e.copy(soT[:, 0:QTS], psAV[:, 0:QTS]),
                          reads=[("ps2",), ("soT",)], writes=[("soT",)])

                        def fin_steps(hh=h, qq=qt, q0f=q0):
                            nqs = QTS // 128
                            st = [
                                lambda: E("T", lambda e: e.matmul(
                                    psR[0:1, 0:QTS], sones[:, 0:1],
                                    sracc[:, 0:QTS], start=True, stop=True),
                                  reads=[("racc",), ("ones",)],
                                  writes=[("ps3",)]),
                                lambda: E("S", lambda e: e.copy(
                                    srow[0:1, 0:QTS], psR[0:1, 0:QTS]),
                                  reads=[("ps3",), ("srow",)],
                                  writes=[("srow",)]),
                                lambda: E("V", lambda e: e.reciprocal(
                                    srow[0:1, 0:QTS], srow[0:1, 0:QTS]),
                                  reads=[("srow",)], writes=[("srow",)]),
                                lambda: E("T", lambda e: e.matmul(
                                    psR[:, 0:QTS], sonesr[0:1, :],
                                    srow[0:1, 0:QTS], start=True, stop=True),
                                  reads=[("srow",), ("onesr",)],
                                  writes=[("ps3",)]),
                                lambda: E("S", lambda e: e.copy(
                                    srbc[:, 0:QTS], psR[:, 0:QTS]),
                                  reads=[("ps3",), ("srbc",)],
                                  writes=[("srbc",)]),
                                lambda: E("V", lambda e: e.scalar_tensor_tensor(
                                    out=soT2[:, 0:QTS], in0=srbc[:, 0:QTS],
                                    scalar=1.0, in1=soT[:, 0:QTS],
                                    op0=AluOpType.mult, op1=AluOpType.mult),
                                  reads=[("soT",), ("srbc",), ("soT2",)],
                                  writes=[("soT2",)]),
                                lambda: E("V", lambda e, hh=hh: e.tensor_scalar(
                                    out=soT2[:, 0:QTS], in0=soT2[:, 0:QTS],
                                    scalar1=sbuv[:, hh:hh + 1], scalar2=None,
                                    op0=AluOpType.add),
                                  reads=[("soT2",), ("buv",)],
                                  writes=[("soT2",)]),
                            ]
                            for i in range(nqs):
                                tp = i % 2
                                st.append(lambda ii=i, t=tp: (
                                    E("T", lambda e: e.transpose(
                                        pt[t][:, 0:128],
                                        soT2[:, ii * 128:(ii + 1) * 128],
                                        sident[:, :]),
                                      reads=[("soT2",), ("ident",)],
                                      writes=[(f"pt{t}",)]),
                                    E("V", lambda e: e.tensor_copy(
                                        sanst[:, ii * 128:(ii + 1) * 128],
                                        pt[t][:, 0:128]),
                                      reads=[(f"pt{t}",), ("anst",)],
                                      writes=[("anst",)])))
                            st.append(lambda: dma("Y", "qAT",
                                attn_hbm[q0f:q0f + QTS, hh * dh:(hh + 1) * dh]
                                .rearrange("(i p) d -> p i d", p=128),
                                sanst[:, 0:QTS].rearrange("p (i d) -> p i d",
                                                          d=128),
                                reads=[("anst",)],
                                writes=[("attn_done", hh, qq)]))
                            return st

                        assert not pending_fin
                        pending_fin.extend(fin_steps())
                    for _ in range(2):
                        if mat_next[0] < n_gu_jobs:
                            plan_mat_job(mat_next[0])
                            mat_next[0] += 1

                while pending_fin:
                    pending_fin.pop(0)()

                while mat_next[0] < n_gu_jobs:
                    plan_mat_job(mat_next[0])
                    mat_next[0] += 1
                while mat_queue:
                    mat_queue.pop(0)()
                wreads = []
                for j, mj in enumerate(mat_jobs):
                    if mj[0] == "gu":
                        wreads += [("gw", j, k) for k in range(mj[4])]
                E("G", lambda e: e.collective_compute(
                    "AllGather", AluOpType.bypass,
                    replica_groups=[list(range(C))],
                    ins=[gw_in.ap().opt()], outs=[gw_out.ap().opt()]),
                  reads=wreads, writes=[("gw_out",)], coll="ccW")
                while mat_next[0] < njobs:
                    plan_mat_job(mat_next[0])
                    mat_next[0] += 1
                while mat_queue:
                    mat_queue.pop(0)()
                dreads = [("gw", j, 0) for j, mj in enumerate(mat_jobs)
                          if mj[0] == "dn"]
                E("G", lambda e: e.collective_compute(
                    "AllGather", AluOpType.bypass,
                    replica_groups=[list(range(C))],
                    ins=[gd_in.ap().opt()], outs=[gd_out.ap().opt()]),
                  reads=dreads, writes=[("gd_out",)], coll="ccD")

        P.barrier(exclude=("ccD",))

        with ExitStack() as phC:
            ce = phC.enter_context
            swg = [ce(nc.sbuf_tensor(f"swg{i}", [128, 2 * DT * 128], BF16))
                   for i in range(2)]
            shT = ce(nc.sbuf_tensor("shT", [128, OT * Tc], BF16))
            sg = ce(nc.sbuf_tensor("sg", [128, 512], BF16))
            swd = [ce(nc.sbuf_tensor(f"swd{i}", [128, NSLOT * C * 128], BF16))
                   for i in range(2)]
            gw4 = gw_out.rearrange("(r t p) q -> p r t q", p=128, t=2 * WG_T)
            gd4 = gd_out.rearrange("(r t p) q -> p r t q", p=128, t=WD_T)
            sfT = ce(nc.sbuf_tensor("sfT", [128, 512], BF16))
            sfn = ce(nc.sbuf_tensor("sfn", [128, (STS // 128) * D], BF16))
            sxc = ce(nc.sbuf_tensor("sxc", [128, D], F32))
            satn = ce(nc.sbuf_tensor("satn", [128, D], BF16))

            nb = HSL // 128
            for ot in range(OT):
                par = ot % 2
                gtok = None
                for mi in range(2):
                    base = mi * WG_T + ot * nb
                    for r in range(C):
                        c0 = (mi * DT + r * nb) * 128
                        gtok = dma("Y", f"qFG{par}",
                                   swg[par][:, c0:c0 + nb * 128]
                                   .rearrange("p (j q) -> p j q", q=128),
                                   gw4[:, r, base:base + nb, :],
                                   reads=[("gw_out",)],
                                   writes=[("swg", par)] if (mi == 0 and r == 0) else [])
                P.set_tok(("swg", par), gtok)
                for st in range(NST):
                    gb = st % 2
                    ub = 2 + st % 2
                    E("T", [
                        (lambda e, s=st, cc=ccc, p=par, fi=(ccc == 0),
                         la=(ccc == DT - 1), b=gb:
                         e.matmul(ps[b][:, 0:STS],
                                  swg[p][:, cc * 128: cc * 128 + 128],
                                  sxnT[:, cc * Tc + s * STS: cc * Tc + s * STS + STS],
                                  start=fi, stop=la))
                        for ccc in range(DT)],
                      reads=[("swg", par)] + xnT_keys, writes=[(f"ps{gb}",)])
                    E("T", [
                        (lambda e, s=st, cc=ccc, p=par, fi=(ccc == 0),
                         la=(ccc == DT - 1), b=ub:
                         e.matmul(ps[b][:, 0:STS],
                                  swg[p][:, (DT + cc) * 128: (DT + cc) * 128 + 128],
                                  sxnT[:, cc * Tc + s * STS: cc * Tc + s * STS + STS],
                                  start=fi, stop=la))
                        for ccc in range(DT)],
                      reads=[("swg", par)] + xnT_keys, writes=[(f"ps{ub}",)])
                    E("S", lambda e, b=gb: e.activation(
                        out=sg[:, 0:STS], in_=ps[b][:, 0:STS], func=AF.Silu),
                      reads=[(f"ps{gb}",), ("sg",)], writes=[("sg",)])
                    E("V", lambda e, o=ot, s=st, b=ub: e.scalar_tensor_tensor(
                        out=shT[:, o * Tc + s * STS: o * Tc + s * STS + STS],
                        in0=ps[b][:, 0:STS], scalar=1.0, in1=sg[:, 0:STS],
                        op0=AluOpType.mult, op1=AluOpType.mult),
                      reads=[(f"ps{ub}",), ("sg",)], writes=[("hT", ot)])

            hT_keys = [("hT", o) for o in range(OT)]
            for st in range(NST):
                for dt in range(DT):
                    par = (st * DT + dt) % 2
                    gtok = None
                    for r in range(C):
                        gtok = dma("Y", f"qFD{par}",
                                   swd[par][:, r * 128:]
                                   .rearrange("p (s q) -> p s q", q=128)
                                   [:, 0:NSLOT * C:C, :] if False else
                                   swd[par][:, :]
                                   .rearrange("p (s r2 q) -> p s r2 q", r2=C, q=128)
                                   [:, :, r, :],
                                   gd4[:, r, dt * NSLOT:(dt + 1) * NSLOT, :],
                                   reads=[("gd_out",)],
                                   writes=[("swd", par)] if r == 0 else [])
                    P.set_tok(("swd", par), gtok)
                    db = dt % 2
                    E("T", [
                        (lambda e, s=st, o=ooo, p=par, fi=(ooo == 0),
                         la=(ooo == OT - 1), b=db:
                         e.matmul(ps[b][:, 0:STS],
                                  swd[p][:, o * 128:(o + 1) * 128],
                                  shT[:, o * Tc + s * STS: o * Tc + s * STS + STS],
                                  start=fi, stop=la))
                        for ooo in range(OT)],
                      reads=[("swd", par)] + hT_keys, writes=[(f"ps{db}",)])
                    E("S", lambda e, b=db: e.copy(sfT[:, 0:STS], ps[b][:, 0:STS]),
                      reads=[(f"ps{db}",), ("sfT",)], writes=[("sfT",)])
                    for i in range(STS // 128):
                        tp = i % 2
                        E("T", lambda e, ii=i, t=tp: e.transpose(
                            pt[t][:, 0:128], sfT[:, ii * 128:(ii + 1) * 128],
                            sident[:, :]),
                          reads=[("sfT",), ("ident",)], writes=[(f"pt{tp}",)])
                        E("V", lambda e, d=dt, ii=i, t=tp: e.tensor_copy(
                            sfn[:, ii * D + d * 128: ii * D + (d + 1) * 128],
                            pt[t][:, 0:128]),
                          reads=[(f"pt{tp}",)], writes=[("fn", i, dt)])
                for ii in range(STS // 128):
                    j = st * (STS // 128) + ii
                    dma("Y", "qCX", sxc[:, :], x_in[j * 128:(j + 1) * 128, :],
                        reads=[("sxc",)], writes=[("sxc",)])
                    dma("Y", "qCA", satn[:, :], attn_hbm[j * 128:(j + 1) * 128, :],
                        reads=[("satn",)], writes=[("satn",)])
                    E("V", lambda e: e.tensor_tensor(
                        out=sxc[:, :], in0=sxc[:, :], in1=satn[:, :],
                        op=AluOpType.add),
                      reads=[("sxc",), ("satn",)], writes=[("sxc",)])
                    E("V", lambda e, i2=ii: e.tensor_tensor(
                        out=sxc[:, :], in0=sxc[:, :],
                        in1=sfn[:, i2 * D:(i2 + 1) * D], op=AluOpType.add),
                      reads=[("sxc",)] + [("fn", ii, d) for d in range(DT)],
                      writes=[("sxc",)])
                    dma("Y", "qY", y_out[j * 128:(j + 1) * 128, :], sxc[:, :],
                        reads=[("sxc",)], writes=[("y", j)])

        P.barrier()

        block = oe(nc.Block())

        @block.tensor
        def _(t):
            P.run("T", t, sems)

        @block.vector
        def _(v):
            P.run("V", v, sems)

        @block.scalar
        def _(s):
            P.run("S", s, sems)

        @block.gpsimd
        def _(g):
            P.run("G", g, sems)

        @block.sync
        def _(y):
            P.run("Y", y, sems)

    return nc



def _host_tables(cfg, core):
    Tc, dr = cfg["Tc"], cfg["dr"]
    t = (np.arange(Tc, dtype=np.float32) + np.float32(core * Tc))
    freqs = (10000.0 ** (-(np.arange(0, dr, 2, dtype=np.float32) / dr))).astype(np.float32)
    theta = freqs[:, None] * t[None, :]
    cos = np.cos(theta).astype(np.float32)
    sin = np.sin(theta).astype(np.float32)
    reps = 128 // (dr // 2)
    cos16 = np.ascontiguousarray(np.tile(cos, (reps, 1)))
    sin16 = np.ascontiguousarray(np.tile(sin, (reps, 1)))
    sinpm = np.ascontiguousarray(np.concatenate([-sin, sin, -sin, sin], axis=0))
    return cos16, sin16, sinpm


def _perm_cols(W, H, dr, swapped):
    cols = []
    for h in range(H):
        base = h * dr
        ev = list(range(base, base + dr, 2))
        od = list(range(base + 1, base + dr, 2))
        cols += (od + ev) if swapped else (ev + od)
    return W[..., cols]


def _perm_cols_grouped(W, H, dr):
    hpg = 128 // (dr // 2)
    cols = []
    for g in range(H // hpg):
        for part in range(2):
            for hh in range(hpg):
                base = (g * hpg + hh) * dr
                cols += list(range(base + part, base + dr, 2))
    return W[..., cols]


def make_in_maps(cfg, inputs):
    C, Tc, H, dr, D, HSL, NSLOT, OT = (cfg["C"], cfg["Tc"], cfg["H"], cfg["dr"],
                                       cfg["D"], cfg["HSL"], cfg["NSLOT"], cfg["OT"])
    x = np.ascontiguousarray(np.asarray(inputs["L0_attn_x"], dtype=np.float32)
                             .reshape(cfg["T"], D))
    f32 = lambda k: np.ascontiguousarray(np.asarray(inputs[k], dtype=np.float32))
    W_QR, W_KR = f32("L0_attn_W_QR"), f32("L0_attn_W_KR")
    b_QR, b_KR = f32("L0_attn_b_QR"), f32("L0_attn_b_KR")
    gidx = np.asarray(inputs["L0_ffn_gate_idx"], dtype=np.int32)
    uidx = np.asarray(inputs["L0_ffn_up_idx"], dtype=np.int32)
    didx = np.asarray(inputs["L0_ffn_down_idx"], dtype=np.int32)
    ident = np.eye(128, dtype=BF)

    w_qr0 = np.ascontiguousarray(_perm_cols(W_QR, H, dr, False))
    w_qr1 = np.ascontiguousarray(_perm_cols(W_QR, H, dr, True))
    b_qr0 = np.ascontiguousarray(_perm_cols(b_QR[None, :], H, dr, False)[0])
    b_qr1 = np.ascontiguousarray(_perm_cols(b_QR[None, :], H, dr, True)[0])
    w_krg = np.ascontiguousarray(_perm_cols_grouped(W_KR, H, dr))
    b_krg = np.ascontiguousarray(_perm_cols_grouped(b_KR[None, :], H, dr)[0])

    in_maps = []
    for c in range(C):
        cos16, sin16, sinpm = _host_tables(cfg, c)
        slots = []
        for s in range(NSLOT):
            ot = c + C * s
            if ot < OT:
                slots.append(didx[:, ot * 128:(ot + 1) * 128])
            else:
                slots.append(np.zeros((D, 128), np.int32))
        didx_c = np.ascontiguousarray(np.concatenate(slots, axis=1))
        pc = lambda v, n: np.ascontiguousarray(v.reshape(n, 128).T)
        in_maps.append({
            "x": np.ascontiguousarray(x[c * Tc:(c + 1) * Tc]),
            "w_dkv": f32("L0_attn_W_DKV"), "w_uk": f32("L0_attn_W_UK"),
            "w_uv": f32("L0_attn_W_UV"), "w_dq": f32("L0_attn_W_DQ"),
            "w_uq": f32("L0_attn_W_UQ"),
            "w_qr0": w_qr0, "w_qr1": w_qr1, "w_krg": w_krg,
            "b_dkv": pc(f32("L0_attn_b_DKV"), cfg["DCT"]),
            "b_uk": pc(f32("L0_attn_b_UK"), cfg["DT"]),
            "b_uv": pc(f32("L0_attn_b_UV"), H),
            "b_dq": pc(f32("L0_attn_b_DQ"), cfg["DCT"]),
            "b_uq": pc(f32("L0_attn_b_UQ"), cfg["DT"]),
            "b_qr0": np.ascontiguousarray(b_qr0.reshape(H // 2, 2 * dr).T),
            "b_qr1": np.ascontiguousarray(b_qr1.reshape(H // 2, 2 * dr).T),
            "b_krg": pc(b_krg, cfg["DCT"]),
            "lut": f32("L0_ffn_lut"),
            "gidx": np.ascontiguousarray(gidx[:, c * HSL:(c + 1) * HSL]),
            "uidx": np.ascontiguousarray(uidx[:, c * HSL:(c + 1) * HSL]),
            "didx": didx_c,
            "cos16": cos16, "sin16": sin16, "sinpm": sinpm,
            "ident": ident,
        })
    return in_maps


def kernel(**inputs):
    cfg = make_cfg(mini=False)
    in_maps = make_in_maps(cfg, inputs)
    nc = build_program(cfg)
    res = run_bass_kernel_spmd(nc, in_maps, core_ids=list(range(cfg["C"])))
    out = np.concatenate([res.results[c]["y"] for c in range(cfg["C"])], axis=0)
    return out.reshape(1, cfg["T"], cfg["D"]).astype(np.float32)


# version 28
# speedup vs baseline: 1.0265x; 1.0265x over previous
import numpy as np
import ml_dtypes
from contextlib import ExitStack

import concourse.bass as bass
import concourse.mybir as mybir
from concourse.bass_utils import run_bass_kernel_spmd
from concourse.alu_op_type import AluOpType

F32 = mybir.dt.float32
BF16 = mybir.dt.bfloat16
I32 = mybir.dt.int32
AF = mybir.ActivationFunctionType

BF = ml_dtypes.bfloat16


def make_cfg(mini=False):
    c = {}
    c["C"] = 8
    c["D"] = 2048
    c["H"] = 16
    c["dh"] = 128
    c["dc"] = 512
    c["dr"] = 32
    c["R"] = 16
    c["eps"] = 1e-5
    if mini:
        c["T"] = 2048
        c["I"] = 1024
    else:
        c["T"] = 8192
        c["I"] = 5504
    c["Tc"] = c["T"] // c["C"]
    c["QTS"] = min(512, c["Tc"])
    c["NQT"] = c["Tc"] // c["QTS"]
    c["KC"] = c["T"] // 128
    c["DT"] = c["D"] // 128
    c["DCT"] = c["dc"] // 128
    c["NLT"] = c["Tc"] // 128
    c["STS"] = min(512, c["Tc"])
    c["NST"] = c["Tc"] // c["STS"]
    c["OT"] = c["I"] // 128
    c["HSL"] = c["D"] // c["C"]
    c["NSLOT"] = (c["OT"] + c["C"] - 1) // c["C"]
    c["WG_T"] = c["OT"] * 2
    c["WD_T"] = c["NSLOT"] * 16
    c["W_TILES"] = 2 * c["WG_T"] + c["WD_T"]
    c["scale"] = float(1.0 / np.sqrt(np.float32(c["dh"] + c["dr"])))
    return c


ENGS = ("T", "V", "S", "G", "Y")
ENG_SEM = {"T": "eT", "V": "eV", "S": "eS", "G": "eG", "Y": None}
ENG_DMA = {"Y": "qS", "S": "qA", "G": "qG"}


class Planner:
    def __init__(self):
        self.thunks = {e: [] for e in ENGS}
        self.cnt = {}
        self.tok = {}
        self.readers = {}
        self.waited = {e: {} for e in ENGS}

    def _wait(self, eng, sem, val):
        if val <= 0:
            return
        if eng == "T" and sem == "eT":
            return
        if self.waited[eng].get(sem, 0) >= val:
            return
        self.waited[eng][sem] = val
        self.thunks[eng].append(("w", sem, val))

    def wait_key(self, eng, key):
        t = self.tok.get(key)
        if t is not None:
            self._wait(eng, t[0], t[1])

    def emit(self, eng, builders, reads=(), writes=(), dma=None, coll=None):
        if callable(builders):
            builders = [builders]
        need = {}
        def _need(t):
            if t is not None and need.get(t[0], 0) < t[1]:
                need[t[0]] = t[1]
        for k in reads:
            _need(self.tok.get(k))
        for k in writes:
            _need(self.tok.get(k))
            for t in self.readers.get(k, ()):
                _need(t)
        for s, v in need.items():
            self._wait(eng, s, v)
        if dma is not None:
            sem, step = dma, 16
        elif coll is not None:
            sem, step = coll, 1
        else:
            sem, step = ENG_SEM[eng], 1
        self.cnt[sem] = self.cnt.get(sem, 0) + step
        tok = (sem, self.cnt[sem])
        for i, b in enumerate(builders):
            last = i == len(builders) - 1
            if not last:
                self.thunks[eng].append(("i", b, None, 0))
            elif coll is not None:
                self.thunks[eng].append(("c", b, sem))
            else:
                self.thunks[eng].append(("i", b, sem, step))
        for k in reads:
            self.readers.setdefault(k, []).append(tok)
        for k in writes:
            self.tok[k] = tok
            self.readers[k] = []
        return tok

    def set_tok(self, key, tok):
        self.tok[key] = tok
        self.readers[key] = []

    def barrier(self, exclude=()):
        snap = dict(self.cnt)
        for e in ENGS:
            for sem, val in snap.items():
                if sem in exclude:
                    continue
                self._wait(e, sem, val)

    def run(self, eng, engobj, sems):
        for th in self.thunks[eng]:
            if th[0] == "w":
                engobj.wait_ge(sems[th[1]], th[2])
            elif th[0] == "i":
                inst = th[1](engobj)
                if th[2] is not None:
                    inst.then_inc(sems[th[2]], th[3])
            else:
                th[1](engobj).then_inc(sems[th[2]])



def build_program(cfg):
    C, D, H, dh, dc, dr, R = (cfg["C"], cfg["D"], cfg["H"], cfg["dh"],
                              cfg["dc"], cfg["dr"], cfg["R"])
    T, Tc, QTS, NQT, KC = cfg["T"], cfg["Tc"], cfg["QTS"], cfg["NQT"], cfg["KC"]
    DT, DCT, NLT, NST, STS = cfg["DT"], cfg["DCT"], cfg["NLT"], cfg["NST"], cfg["STS"]
    OT, HSL, NSLOT = cfg["OT"], cfg["HSL"], cfg["NSLOT"]
    WG_T, WD_T, W_TILES = cfg["WG_T"], cfg["WD_T"], cfg["W_TILES"]
    I_ = cfg["I"]
    eps = cfg["eps"]
    scale = cfg["scale"]
    KROWS = D + dc
    HPG = 128 // (dr // 2)

    nc = bass.Bass("TRN2", target_bir_lowering=False)

    x_in = nc.dram_tensor("x", [Tc, D], F32, kind="ExternalInput")
    w_dkv = nc.dram_tensor("w_dkv", [D, dc], F32, kind="ExternalInput")
    w_uk = nc.dram_tensor("w_uk", [dc, D], F32, kind="ExternalInput")
    w_uv = nc.dram_tensor("w_uv", [dc, D], F32, kind="ExternalInput")
    w_dq = nc.dram_tensor("w_dq", [D, dc], F32, kind="ExternalInput")
    w_uq = nc.dram_tensor("w_uq", [dc, D], F32, kind="ExternalInput")
    w_qr0 = nc.dram_tensor("w_qr0", [dc, dc], F32, kind="ExternalInput")
    w_qr1 = nc.dram_tensor("w_qr1", [dc, dc], F32, kind="ExternalInput")
    w_krg = nc.dram_tensor("w_krg", [D, dc], F32, kind="ExternalInput")
    b_dkv = nc.dram_tensor("b_dkv", [128, DCT], F32, kind="ExternalInput")
    b_uk = nc.dram_tensor("b_uk", [128, DT], F32, kind="ExternalInput")
    b_uv = nc.dram_tensor("b_uv", [128, H], F32, kind="ExternalInput")
    b_dq = nc.dram_tensor("b_dq", [128, DCT], F32, kind="ExternalInput")
    b_uq = nc.dram_tensor("b_uq", [128, DT], F32, kind="ExternalInput")
    b_qr0 = nc.dram_tensor("b_qr0", [2 * dr, H // 2], F32, kind="ExternalInput")
    b_qr1 = nc.dram_tensor("b_qr1", [2 * dr, H // 2], F32, kind="ExternalInput")
    b_krg = nc.dram_tensor("b_krg", [128, DCT], F32, kind="ExternalInput")
    lut_in = nc.dram_tensor("lut", [16], F32, kind="ExternalInput")
    gidx_in = nc.dram_tensor("gidx", [I_, HSL], I32, kind="ExternalInput")
    uidx_in = nc.dram_tensor("uidx", [I_, HSL], I32, kind="ExternalInput")
    didx_in = nc.dram_tensor("didx", [D, NSLOT * 128], I32, kind="ExternalInput")
    cos16_in = nc.dram_tensor("cos16", [128, Tc], F32, kind="ExternalInput")
    sin16_in = nc.dram_tensor("sin16", [128, Tc], F32, kind="ExternalInput")
    sinpm_in = nc.dram_tensor("sinpm", [4 * R, Tc], F32, kind="ExternalInput")
    ident_in = nc.dram_tensor("ident", [128, 128], BF16, kind="ExternalInput")
    y_out = nc.dram_tensor("y", [Tc, D], F32, kind="ExternalOutput")

    gk_in = nc.dram_tensor("gk_in", [KROWS, Tc], BF16)
    gk_out = nc.dram_tensor("gk_out", [C * KROWS, Tc], BF16)
    gv_in = nc.dram_tensor("gv_in", [H * Tc, dh], BF16)
    gv_out = nc.dram_tensor("gv_out", [C * H * Tc, dh], BF16)
    gw_in = nc.dram_tensor("gw_in", [2 * WG_T * 128, 128], BF16)
    gw_out = nc.dram_tensor("gw_out", [C * 2 * WG_T * 128, 128], BF16)
    gd_in = nc.dram_tensor("gd_in", [WD_T * 128, 128], BF16)
    gd_out = nc.dram_tensor("gd_out", [C * WD_T * 128, 128], BF16)
    attn_hbm = nc.dram_tensor("attn_hbm", [Tc, D], BF16)
    rs_hbm = nc.dram_tensor("rs_hbm", [512], F32)

    P = Planner()
    E = P.emit

    with ExitStack() as outer:
        oe = outer.enter_context

        sems = {}
        for name in ("eT", "eV", "eS", "eG",
                     "qST", "qS2", "qX", "qB", "qWB00", "qWB01", "qWB10", "qWB11", "qGI",
                     "qSK0", "qSK1", "qKR", "qAT",
                     *[f"qK{i}" for i in range(8)],
                     *[f"qR{i}" for i in range(8)],
                     *[f"qV{i}" for i in range(8)],
                     "qAW0", "qAW1", "qFG0", "qFG1", "qFD0", "qFD1",
                     "qCX", "qCA", "qY",
                     "ccK", "ccV", "ccW", "ccD"):
            sems[name] = oe(nc.semaphore(name))

        sxnT = oe(nc.sbuf_tensor("sxnT", [128, DT * Tc], BF16))
        sident = oe(nc.sbuf_tensor("sident", [128, 128], BF16))
        sones = oe(nc.sbuf_tensor("sones", [128, 1], F32))
        sonesb = oe(nc.sbuf_tensor("sonesb", [128, 1], BF16))
        sonesr = oe(nc.sbuf_tensor("sonesr", [1, 128], F32))
        slut = oe(nc.sbuf_tensor("slut", [1, 16], F32))
        slutbc = oe(nc.sbuf_tensor("slutbc", [128, 16], F32))

        psS = [oe(nc.psum_tensor(f"psS{i}", [128, 1024], F32)) for i in range(2)]
        psAV = oe(nc.psum_tensor("psAV", [128, 512], F32))
        psR = oe(nc.psum_tensor("psR", [128, 512], F32))
        pt = [oe(nc.psum_tensor(f"pt{i}", [128, 1024], BF16)) for i in range(2)]
        ps = [psS[0][:, 0:512], psS[1][:, 0:512], psAV[:, :], psR[:, :]]

        def dma(eng, q, dst, src, reads=(), writes=()):
            return E(eng, lambda e, d=dst, s=src: e.dma_start(out=d, in_=s),
                     reads=reads, writes=writes, dma=q)

        dma("Y", "qST", sident[:, :], ident_in[:, :], writes=[("ident",)])
        dma("Y", "qST", slut[0:1, :], lut_in[None, :], writes=[("slut",)])
        for _k in (("ident",), ("slut",)):
            P.tok[_k] = ("qST", P.cnt["qST"])
        E("V", lambda e: e.memset(sones[:, :], 1.0), writes=[("ones",)])
        E("V", lambda e: e.memset(sonesb[:, :], 1.0), writes=[("onesb",)])
        E("V", lambda e: e.memset(sonesr[:, :], 1.0), writes=[("onesr",)])
        E("T", lambda e: e.matmul(ps[2][:, 0:16], sonesr[0:1, :], slut[0:1, :],
                                  start=True, stop=True),
          reads=[("onesr",), ("slut",)], writes=[("ps2",)])
        E("V", lambda e: e.tensor_copy(slutbc[:, :], ps[2][:, 0:16]),
          reads=[("ps2",)], writes=[("lutbc",)])

        xnT_keys = [("xnT", cc) for cc in range(DT)]

        with ExitStack() as phAB:
            abe = phAB.enter_context
            sQcT = abe(nc.sbuf_tensor("sQcT", [128, H * Tc], BF16))
            sQrT = abe(nc.sbuf_tensor("sQrT", [128, H * Tc], BF16))
            MJF = 1024
            smidx = abe(nc.sbuf_tensor("smidx", [128, MJF], I32))
            smbf = [abe(nc.sbuf_tensor(f"smbf{i}", [128, MJF], BF16)) for i in range(2)]
            smacc = [abe(nc.sbuf_tensor(f"smacc{i}", [128, MJF], BF16)) for i in range(2)]
            smtmp = abe(nc.sbuf_tensor("smtmp", [128, MJF], BF16))
            swst = [abe(nc.sbuf_tensor(f"swst{i}", [128, MJF], BF16)) for i in range(2)]


            with ExitStack() as phA:
                ae = phA.enter_context
                scos = ae(nc.sbuf_tensor("scos", [128, Tc], BF16))
                ssin = ae(nc.sbuf_tensor("ssin", [128, Tc], BF16))
                ssinpm = ae(nc.sbuf_tensor("ssinpm", [4 * R, Tc], BF16))
                swstage = [ae(nc.sbuf_tensor(f"swstage{i}", [128, 1024], F32))
                           for i in range(2)]
                s_x = ae(nc.sbuf_tensor("s_x", [128, D], F32))
                s_xn = ae(nc.sbuf_tensor("s_xn", [128, D], BF16))
                s_st = ae(nc.sbuf_tensor("s_st", [128, 32], F32))
                s_mv = ae(nc.sbuf_tensor("s_mv", [128, 4], F32))
                sckvT = ae(nc.sbuf_tensor("sckvT", [128, DCT * Tc], BF16))
                scqT = ae(nc.sbuf_tensor("scqT", [128, DCT * Tc], BF16))
                swbufs = [ae(nc.sbuf_tensor(f"swbuf{i}", [128, DT * 512], BF16)) for i in range(2)]
                sbias = ae(nc.sbuf_tensor("sbias", [128, 32], F32))
                sbq = ae(nc.sbuf_tensor("sbq", [128, DT], F32))
                sbqr = ae(nc.sbuf_tensor("sbqr", [2 * dr, H], F32))
                skst = [ae(nc.sbuf_tensor(f"skst{i}", [128, 512], BF16)) for i in range(2)]
                skrp = [ae(nc.sbuf_tensor(f"skrp{i}", [128, Tc], BF16)) for i in range(2)]
                sk1 = ae(nc.sbuf_tensor("sk1", [128, Tc], F32))
                sk2 = ae(nc.sbuf_tensor("sk2", [128, Tc], F32))
                sqrp = [ae(nc.sbuf_tensor(f"sqrp{i}", [4 * R, Tc], BF16)) for i in range(2)]

                dma("G", "qS2", scos[:, :], cos16_in[:, :], writes=[("cos",)])
                dma("G", "qS2", ssin[:, :], sin16_in[:, :], writes=[("sin",)])
                dma("G", "qS2", ssinpm[:, :], sinpm_in[:, :], writes=[("sinpm",)])
                for _k in (("cos",), ("sin",), ("sinpm",)):
                    P.tok[_k] = ("qS2", P.cnt["qS2"])
                for i in range(NLT):
                    dma("Y", "qX", s_x[:, :], x_in[i * 128:(i + 1) * 128, :],
                        reads=[("x",)], writes=[("x",)])
                    nbn = D // 512
                    for bnc in range(nbn):
                        E("V", lambda e, q=bnc: e.bn_stats(
                            s_st[:, q * 6:q * 6 + 6],
                            s_x[:, q * 512:(q + 1) * 512]),
                          reads=[("x",), ("st",)], writes=[("st",)])
                    E("V", lambda e, nn=nbn: e.bn_aggr(s_mv[:, 0:2], s_st[:, 0:6 * nn]),
                      reads=[("st",), ("mv",)], writes=[("mv",)])
                    E("V", lambda e: e.tensor_scalar(out=s_mv[:, 2:3], in0=s_mv[:, 1:2],
                                                     scalar1=float(eps), scalar2=None,
                                                     op0=AluOpType.add),
                      reads=[("mv",)], writes=[("veps",)])
                    E("S", lambda e: e.activation(out=s_st[:, 30:31], in_=s_mv[:, 2:3],
                                                  func=AF.Sqrt),
                      reads=[("veps",), ("sqv",)], writes=[("sqv",)])
                    E("V", lambda e: e.reciprocal(s_mv[:, 2:3], s_st[:, 30:31]),
                      reads=[("sqv",), ("veps",)], writes=[("rstd",)])
                    E("V", lambda e: e.tensor_tensor(out=s_mv[:, 3:4], in0=s_mv[:, 0:1],
                                                     in1=s_mv[:, 2:3], op=AluOpType.mult),
                      reads=[("rstd",), ("mv",), ("nmrs",)], writes=[("mrs",)])
                    E("V", lambda e: e.tensor_scalar(out=s_mv[:, 3:4], in0=s_mv[:, 3:4],
                                                     scalar1=-1.0, scalar2=None,
                                                     op0=AluOpType.mult),
                      reads=[("mrs",)], writes=[("nmrs",)])
                    E("S", lambda e: e.activation(out=s_xn[:, :], in_=s_x[:, :],
                                                  func=AF.Identity,
                                                  bias=s_mv[:, 3:4],
                                                  scale=s_mv[:, 2:3]),
                      reads=[("x",), ("rstd",), ("nmrs",), ("xn",)], writes=[("xn",)])
                    for c in range(DT):
                        tp = c % 2
                        E("T", lambda e, cc=c, t=tp: e.transpose(
                            pt[t][:, 0:128], s_xn[:, cc * 128:(cc + 1) * 128],
                            sident[:, :]),
                          reads=[("xn",), ("ident",)], writes=[(f"pt{tp}",)])
                        E("S", lambda e, cc=c, ii=i, t=tp: e.copy(
                            sxnT[:, cc * Tc + ii * 128: cc * Tc + (ii + 1) * 128],
                            pt[t][:, 0:128]),
                          reads=[(f"pt{tp}",)], writes=[("xnT", c)])

                dma("Y", "qB", sbias[:, 0:DCT], b_dkv[:, :], writes=[("bias",)])
                dma("Y", "qB", sbias[:, 4:4 + DCT], b_dq[:, :], writes=[("bias",)])
                dma("Y", "qB", sbias[:, 8:8 + DCT], b_krg[:, :], writes=[("bias",)])
                dma("Y", "qB", sbias[:, 12:12 + DT], b_uk[:, :], writes=[("bias",)])
                dma("Y", "qB", sbq[:, :], b_uq[:, :], writes=[("biasq",)])
                dma("Y", "qB", sbqr[:, 0:H // 2], b_qr0[:, :], writes=[("biasqr",)])
                dma("Y", "qB", sbqr[:, H // 2:H], b_qr1[:, :], writes=[("biasqr",)])

                for _k in (("bias",), ("biasq",), ("biasqr",)):
                    P.tok[_k] = ("qB", P.cnt["qB"])

                ckv_keys = [("ckvT", cc) for cc in range(DCT)]

                def wload(wsrc, slot, ncols):
                    wb = swbufs[slot]
                    wkey = ("swbuf", slot)
                    nch = wsrc.shape[0] // 128
                    m = wsrc.shape[1]
                    src3 = wsrc.rearrange("(c p) m -> p c m", p=128)
                    pieces_per_chunk = max(1, m // 1024)
                    pw = m // pieces_per_chunk
                    first = True
                    for ccx in range(nch):
                        for pz in range(pieces_per_chunk):
                            par = (ccx * pieces_per_chunk + pz) % 2
                            dma("Y", f"qWB{slot}{par}",
                                swstage[par][:, 0:pw],
                                src3[:, ccx, pz * pw:(pz + 1) * pw],
                                reads=[("wstg", par)], writes=[("wstg", par)])
                            E("V", lambda e, p=par, c=ccx, z=pz, w=wb, ww=pw:
                              e.tensor_copy(
                                  w[:, c * m + z * ww: c * m + (z + 1) * ww],
                                  swstage[p][:, 0:ww]),
                              reads=[("wstg", par)] + ([wkey] if first else []),
                              writes=[wkey])
                            first = False
                cq_keys = [("cqT", cc) for cc in range(DCT)]

                def down_proj(wsrc, out_sb, out_keyname, biascol0, bias_sb, slot):
                    wb = swbufs[slot]
                    wkey = ("swbuf", slot)
                    wload(wsrc, slot, 512)
                    for mt in range(DCT):
                        for st in range(NST):
                            bk = (mt * NST + st) % 2
                            E("T", [
                                (lambda e, m=mt, s=st, cc=ccc, fi=(ccc == 0),
                                 la=(ccc == DT - 1), b=bk:
                                 e.matmul(ps[b][:, 0:STS],
                                          wb[:, cc * 512 + m * 128:
                                             cc * 512 + m * 128 + 128],
                                          sxnT[:, cc * Tc + s * STS:
                                               cc * Tc + s * STS + STS],
                                          start=fi, stop=la))
                                for ccc in range(DT)],
                              reads=[wkey] + xnT_keys, writes=[(f"ps{bk}",)])
                            if out_keyname == "krp":
                                E("S", lambda e, m=mt, s=st, b=bk: e.activation(
                                    out=skrp[m % 2][:, s * STS: s * STS + STS],
                                    in_=ps[b][:, 0:STS], func=AF.Identity,
                                    bias=bias_sb[:, biascol0 + m:biascol0 + m + 1],
                                    scale=1.0),
                                  reads=[(f"ps{bk}",), ("bias",), ("krp", mt % 2)],
                                  writes=[("krp", mt)])
                            else:
                                E("S", lambda e, m=mt, s=st, b=bk: e.activation(
                                    out=out_sb[:, m * Tc + s * STS:
                                               m * Tc + s * STS + STS],
                                    in_=ps[b][:, 0:STS], func=AF.Identity,
                                    bias=bias_sb[:, biascol0 + m:biascol0 + m + 1],
                                    scale=1.0),
                                  reads=[(f"ps{bk}",), ("bias",)],
                                  writes=[(out_keyname, mt)])

                down_proj(w_dkv, sckvT, "ckvT", 0, sbias, 0)
                down_proj(w_dq, scqT, "cqT", 4, sbias, 1)

                wb, wkey = swbufs[0], ("swbuf", 0)
                wload(w_krg, 0, 512)
                for g in range(DCT // 2):
                    for loc in range(2):
                        gt = 2 * g + loc
                        for st in range(NST):
                            bk = (gt * NST + st) % 2
                            E("T", [
                                (lambda e, m=gt, s=st, cc=ccc, fi=(ccc == 0),
                                 la=(ccc == DT - 1), b=bk, w=wb:
                                 e.matmul(ps[b][:, 0:STS],
                                          w[:, cc * 512 + m * 128:
                                            cc * 512 + m * 128 + 128],
                                          sxnT[:, cc * Tc + s * STS:
                                               cc * Tc + s * STS + STS],
                                          start=fi, stop=la))
                                for ccc in range(DT)],
                              reads=[wkey] + xnT_keys, writes=[(f"ps{bk}",)])
                            E("S", lambda e, m=gt, s=st, b=bk, l=loc: e.activation(
                                out=skrp[l][:, s * STS: s * STS + STS],
                                in_=ps[b][:, 0:STS], func=AF.Identity,
                                bias=sbias[:, 8 + m:9 + m], scale=1.0),
                              reads=[(f"ps{bk}",), ("bias",), ("krp", loc)],
                              writes=[("krp", loc)])
                    x1k, x2k = ("krp", 0), ("krp", 1)
                    E("V", lambda e: e.tensor_tensor(out=sk1[:, :], in0=skrp[0][:, :],
                                                     in1=scos[:, :], op=AluOpType.mult),
                      reads=[x1k, ("cos",), ("sk1",)], writes=[("sk1",)])
                    E("V", lambda e: e.tensor_tensor(out=sk2[:, :], in0=skrp[1][:, :],
                                                     in1=ssin[:, :], op=AluOpType.mult),
                      reads=[x2k, ("sin",), ("sk2",)], writes=[("sk2",)])
                    E("V", lambda e: e.tensor_tensor(out=smtmp[:, 0:Tc], in0=sk1[:, :],
                                                     in1=sk2[:, :], op=AluOpType.subtract),
                      reads=[("sk1",), ("sk2",), ("mtmp",)], writes=[("mtmp",)])
                    dma("Y", "qKR", gk_in[D + 2 * g * 128: D + 2 * g * 128 + 128, :],
                        smtmp[:, 0:Tc], reads=[("mtmp",)], writes=[("gk_rope", 2 * g)])
                    E("V", lambda e: e.tensor_tensor(out=sk1[:, :], in0=skrp[1][:, :],
                                                     in1=scos[:, :], op=AluOpType.mult),
                      reads=[x2k, ("cos",), ("sk1",)], writes=[("sk1",)])
                    E("V", lambda e: e.tensor_tensor(out=sk2[:, :], in0=skrp[0][:, :],
                                                     in1=ssin[:, :], op=AluOpType.mult),
                      reads=[x1k, ("sin",), ("sk2",)], writes=[("sk2",)])
                    E("V", lambda e: e.tensor_tensor(out=smtmp[:, 0:Tc], in0=sk1[:, :],
                                                     in1=sk2[:, :], op=AluOpType.add),
                      reads=[("sk1",), ("sk2",), ("mtmp",)], writes=[("mtmp",)])
                    dma("Y", "qKR", gk_in[D + (2 * g + 1) * 128: D + (2 * g + 1) * 128 + 128, :],
                        smtmp[:, 0:Tc], reads=[("mtmp",)], writes=[("gk_rope", 2 * g + 1)])

                wb, wkey = swbufs[1], ("swbuf", 1)
                wload(w_uk, 1, D)
                for dt in range(DT):
                    for st in range(NST):
                        bk = (dt * NST + st) % 2
                        sp = bk
                        E("T", [
                            (lambda e, m=dt, s=st, cc=ccc, fi=(ccc == 0),
                             la=(ccc == DCT - 1), b=bk, w=wb:
                             e.matmul(ps[b][:, 0:STS],
                                      w[:, cc * D + m * 128: cc * D + m * 128 + 128],
                                      sckvT[:, cc * Tc + s * STS: cc * Tc + s * STS + STS],
                                      start=fi, stop=la))
                            for ccc in range(DCT)],
                          reads=[wkey] + ckv_keys, writes=[(f"ps{bk}",)])
                        E("S", lambda e, m=dt, b=bk, p=sp: e.activation(
                            out=skst[p][:, 0:STS], in_=ps[b][:, 0:STS],
                            func=AF.Identity, bias=sbias[:, 12 + m:13 + m], scale=1.0),
                          reads=[(f"ps{bk}",), ("bias",), ("kst", sp)],
                          writes=[("kst", sp)])
                        dma("Y", f"qSK{sp}",
                            gk_in[dt * 128:(dt + 1) * 128, st * STS:st * STS + STS],
                            skst[sp][:, 0:STS], reads=[("kst", sp)],
                            writes=[("gk_c", dt, st)])

                wb, wkey = swbufs[0], ("swbuf", 0)
                wload(w_uv, 0, D)
                for dd in range(D // 512):
                    for st2 in range(NLT):
                        bk = (st2 * (D // 512) + dd) % 2
                        sp = bk
                        blds = [
                            (lambda e, s=st2, d=dd, cc=ccc, fi=(ccc == 0),
                             la=(ccc == DCT - 1), b=bk, w=wb:
                             e.matmul(ps[b][:, 0:512],
                                      sckvT[:, cc * Tc + s * 128: cc * Tc + s * 128 + 128],
                                      w[:, cc * D + d * 512: cc * D + (d + 1) * 512],
                                      start=fi, stop=la))
                            for ccc in range(DCT)]
                        E("T", blds,
                          reads=[wkey] + ckv_keys,
                          writes=[(f"ps{bk}",)])
                        E("S", lambda e, b=bk, p=sp: e.copy(
                            skst[p][:, 0:512], ps[b][:, 0:512]),
                          reads=[(f"ps{bk}",), ("kst", sp)], writes=[("kst", sp)])
                        for hh in range(4):
                            h = dd * 4 + hh
                            dma("Y", f"qSK{sp}",
                                gv_in[h * Tc + st2 * 128: h * Tc + (st2 + 1) * 128, :],
                                skst[sp][:, hh * 128:(hh + 1) * 128],
                                reads=[("kst", sp)], writes=[("gv", h, st2)])

                kreads = ([("gk_c", dt, st) for dt in range(DT) for st in range(NST)] +
                          [("gk_rope", i) for i in range(DCT)])
                E("G", lambda e: e.collective_compute(
                    "AllGather", AluOpType.bypass,
                    replica_groups=[list(range(C))],
                    ins=[gk_in.ap().opt()], outs=[gk_out.ap().opt()]),
                  reads=kreads, writes=[("gk_out",)], coll="ccK")
                vreads = [("gv", h, s2) for h in range(H) for s2 in range(NLT)]
                E("G", lambda e: e.collective_compute(
                    "AllGather", AluOpType.bypass,
                    replica_groups=[list(range(C))],
                    ins=[gv_in.ap().opt()], outs=[gv_out.ap().opt()]),
                  reads=vreads, writes=[("gv_out",)], coll="ccV")

                wb, wkey = swbufs[1], ("swbuf", 1)
                wload(w_uq, 1, D)
                for dt in range(DT):
                    for st in range(NST):
                        bk = (dt * NST + st) % 2
                        E("T", [
                            (lambda e, m=dt, s=st, cc=ccc, fi=(ccc == 0),
                             la=(ccc == DCT - 1), b=bk, w=wb:
                             e.matmul(ps[b][:, 0:STS],
                                      w[:, cc * D + m * 128: cc * D + m * 128 + 128],
                                      scqT[:, cc * Tc + s * STS: cc * Tc + s * STS + STS],
                                      start=fi, stop=la))
                            for ccc in range(DCT)],
                          reads=[wkey] + cq_keys, writes=[(f"ps{bk}",)])
                        E("S", lambda e, m=dt, s=st, b=bk: e.activation(
                            out=sQcT[:, m * Tc + s * STS: m * Tc + s * STS + STS],
                            in_=ps[b][:, 0:STS], func=AF.Identity,
                            bias=sbq[:, m:m + 1], scale=1.0),
                          reads=[(f"ps{bk}",), ("biasq",)], writes=[("QcT",)])

                E("V", lambda e: e.memset(sQrT[:, :], 0.0), writes=[("QrT",)])
                wb, wkey = swbufs[0], ("swbuf", 0)
                wload(w_qr0, 0, dc)
                src3b = w_qr1.rearrange("(c p) m -> p c m", p=128)
                for ccx in range(DCT):
                    par = ccx % 2
                    dma("Y", f"qWB0{par}", swstage[par][:, 0:dc],
                        src3b[:, ccx, :],
                        reads=[("wstg", par)], writes=[("wstg", par)])
                    E("V", lambda e, p=par, c=ccx: e.tensor_copy(
                        wb[:, DCT * dc + c * dc: DCT * dc + (c + 1) * dc],
                        swstage[p][:, 0:dc]),
                      reads=[("wstg", par), wkey], writes=[wkey])
                for hp in range(H // 2):
                    h2 = 2 * hp
                    for v in range(2):
                        off = (0 if v == 0 else DCT * dc)
                        pp = ps[2] if v == 0 else ps[3]
                        pk = "ps2" if v == 0 else "ps3"
                        for st in range(NST):
                            E("T", [
                                (lambda e, s=st, cc=ccc, fi=(ccc == 0),
                                 la=(ccc == DCT - 1), o=off, q=pp, hh2=h2, w=wb:
                                 e.matmul(q[0:2 * dr, 0:STS],
                                          w[:, o + cc * dc + hh2 * dr:
                                            o + cc * dc + (hh2 + 2) * dr],
                                          scqT[:, cc * Tc + s * STS:
                                               cc * Tc + s * STS + STS],
                                          start=fi, stop=la))
                                for ccc in range(DCT)],
                              reads=[wkey] + cq_keys,
                              writes=[(pk,)])
                            E("S", lambda e, s=st, vv=v, q=pp, hpp=hp: e.activation(
                                out=sqrp[vv][:, s * STS: s * STS + STS],
                                in_=q[0:2 * dr, 0:STS], func=AF.Identity,
                                bias=sbqr[:, vv * (H // 2) + hpp:
                                          vv * (H // 2) + hpp + 1], scale=1.0),
                              reads=[(pk,), ("biasqr",), ("qrp", v)],
                              writes=[("qrp", v)])
                    E("V", lambda e: e.tensor_tensor(
                        out=sk1[0:4 * R, :], in0=sqrp[0][:, :], in1=scos[0:4 * R, :],
                        op=AluOpType.mult),
                      reads=[("qrp", 0), ("cos",), ("sk1",)], writes=[("sk1",)])
                    E("V", lambda e: e.tensor_tensor(
                        out=sk2[0:4 * R, :], in0=sqrp[1][:, :], in1=ssinpm[:, :],
                        op=AluOpType.mult),
                      reads=[("qrp", 1), ("sinpm",), ("sk2",)], writes=[("sk2",)])
                    for dh2 in range(2):
                        E("V", lambda e, hh=h2 + dh2, o=dh2 * 2 * R: e.tensor_tensor(
                            out=sQrT[0:2 * R, hh * Tc:(hh + 1) * Tc],
                            in0=sk1[o:o + 2 * R, :] if o == 0 else sk1[2 * R:4 * R, :],
                            in1=sk2[o:o + 2 * R, :] if o == 0 else sk2[2 * R:4 * R, :],
                            op=AluOpType.add),
                          reads=[("sk1",), ("sk2",)], writes=[("QrT",)])

            mat_jobs = []
            OTJ = MJF // HSL
            for mat_i, src in ((0, gidx_in), (1, uidx_in)):
                ot0 = 0
                while ot0 < OT:
                    n = min(OTJ, OT - ot0)
                    mat_jobs.append(("gu", mat_i, src, ot0, n))
                    ot0 += n
            n_gu_jobs = len(mat_jobs)
            DTJ = MJF // 128
            for slot in range(NSLOT):
                for half in range((DT + DTJ - 1) // DTJ):
                    mat_jobs.append(("dn", 2, didx_in, (slot, half), DTJ))
            njobs = len(mat_jobs)

            mat_queue = []

            def plan_mat_job(jid):
                kind, mi, src_t, a0, n = mat_jobs[jid]
                par = jid % 2
                steps = []
                if kind == "gu":
                    fd = n * HSL
                    steps.append(lambda f=fd, a=a0, nn=n, s=src_t: dma(
                        "G", "qGI", smidx[:, 0:f].rearrange("p (o h) -> p o h", h=HSL),
                        s.rearrange("(o p) h -> p o h", p=128)[:, a:a + nn, :],
                        reads=[("midx",)], writes=[("midx",)]))
                else:
                    slot, half = a0
                    fd = n * 128
                    steps.append(lambda f=fd, sl=slot, hf=half, s=src_t: dma(
                        "G", "qGI", smidx[:, 0:f].rearrange("p (t o) -> p t o", o=128),
                        s.rearrange("(t p) o -> p t o", p=128)
                        [:, hf * DTJ:(hf + 1) * DTJ, sl * 128:(sl + 1) * 128],
                        reads=[("midx",)], writes=[("midx",)]))
                steps.append(lambda p=par, f=fd: E(
                    "V", lambda e: e.tensor_copy(smbf[p][:, 0:f], smidx[:, 0:f]),
                    reads=[("midx",), ("mbf", p)], writes=[("mbf", p)]))
                for j in range(16):
                    def sel_step(p=par, f=fd, jj=j):
                        E("V", lambda e: e.tensor_scalar(
                            out=smtmp[:, 0:f], in0=smbf[p][:, 0:f],
                            scalar1=float(jj), scalar2=slutbc[:, jj:jj + 1],
                            op0=AluOpType.is_equal, op1=AluOpType.mult),
                          reads=[("mbf", p), ("lutbc",), ("mtmp",)],
                          writes=[("mtmp",)])
                        if jj == 0:
                            E("V", lambda e: e.tensor_copy(
                                smacc[p][:, 0:f], smtmp[:, 0:f]),
                              reads=[("mtmp",), ("macc", p)], writes=[("macc", p)])
                        else:
                            E("V", lambda e: e.tensor_tensor(
                                out=smacc[p][:, 0:f], in0=smacc[p][:, 0:f],
                                in1=smtmp[:, 0:f], op=AluOpType.add),
                              reads=[("mtmp",), ("macc", p)], writes=[("macc", p)])
                    steps.append(sel_step)
                for _sp in range(6):
                    steps.append(lambda: None)
                nblk = fd // 128
                def tr_step(p=par, nb2=nblk):
                    for bb in range(nb2):
                        tp = bb % 2
                        E("T", lambda e, b=bb, t=tp: e.transpose(
                            pt[t][:, 0:128], smacc[p][:, b * 128:(b + 1) * 128],
                            sident[:, :]),
                          reads=[("macc", p), ("ident",)], writes=[(f"pt{tp}",)])
                        E("S", lambda e, b=bb, t=tp: e.copy(
                            swst[p][:, b * 128:(b + 1) * 128], pt[t][:, 0:128]),
                          reads=[(f"pt{tp}",), ("wst", p)], writes=[("wst", p)])
                steps.append(tr_step)
                if kind == "gu":
                    nb = HSL // 128
                    def st_step(p=par, m=mi, a=a0, nn=n, jj=jid):
                        for j2 in range(nn):
                            t0 = (m * WG_T + (a + j2) * nb) * 128
                            dma("S", f"qAW{p}", gw_in[t0:t0 + nb * 128, :]
                                .rearrange("(j p) q -> p j q", p=128),
                                swst[p][:, j2 * HSL: j2 * HSL + HSL]
                                .rearrange("p (j q) -> p j q", q=128),
                                reads=[("wst", p)], writes=[("gw", jj, j2)])
                    steps.append(st_step)
                else:
                    slot, half = a0
                    def st_step(p=par, sl=slot, hf=half, nn=n, jj=jid):
                        dview = gd_in[0:DT * NSLOT * 128, :]\
                            .rearrange("(d s p) q -> p d s q", p=128, s=NSLOT)
                        dma("S", f"qAW{p}",
                            dview[:, hf * DTJ:(hf + 1) * DTJ, sl, :],
                            swst[p][:, 0:nn * 128]
                            .rearrange("p (t q) -> p t q", q=128),
                            reads=[("wst", p)], writes=[("gw", jj, 0)])
                    steps.append(st_step)
                mat_queue.extend(steps)

            for _j in range(min(4, njobs)):
                plan_mat_job(_j)
            mat_next = [min(4, njobs)]

            P.barrier()

            with ExitStack() as phB:
                be = phB.enter_context
                sKc = be(nc.sbuf_tensor("sKc", [128, T], BF16))
                sKr = be(nc.sbuf_tensor("sKr", [128, T], BF16))
                sV = be(nc.sbuf_tensor("sV", [128, T], BF16))
                sP = [be(nc.sbuf_tensor(f"sP{i}", [128, 1024], BF16)) for i in range(6)]
                sracc_v = be(nc.sbuf_tensor("sracc_v", [128, 1024], F32))
                sracc_g = be(nc.sbuf_tensor("sracc_g", [128, 1024], F32))
                sracc2 = be(nc.sbuf_tensor("sracc2", [128, 1024], F32))
                sracc = be(nc.sbuf_tensor("sracc", [128, 512], F32))
                srow = be(nc.sbuf_tensor("srow", [1, 512], F32))
                sbuv = be(nc.sbuf_tensor("sbuv", [128, H], F32))
                srbc = be(nc.sbuf_tensor("srbc", [128, 512], F32))
                soT2 = be(nc.sbuf_tensor("soT2", [128, 512], BF16))
                soT = be(nc.sbuf_tensor("soT", [128, 512], BF16))
                sanst = be(nc.sbuf_tensor("sanst", [128, 512], BF16))

                gkr = gk_out.rearrange("(r k) s -> k r s", k=KROWS)
                dma("Y", "qB", sbuv[:, :], b_uv[:, :], writes=[("buv",)])
                P.tok[("buv",)] = ("qB", P.cnt["qB"])
                E("V", lambda e: e.memset(sKr[:, :], 0.0), writes=[("Kr", r) for r in range(C)])
                pending_fin = []

                for h in range(H):
                    g, hl = h // HPG, h % HPG
                    b1 = D + 2 * g * 128 + hl * (dr // 2)
                    b2 = b1 + 128
                    for r in range(C):
                        dma("Y", f"qK{r}", sKc[:, r * Tc:(r + 1) * Tc],
                            gkr[h * 128:(h + 1) * 128, r, :],
                            reads=[("gk_out",), ("Kc", r)], writes=[("Kc", r)])
                    for r in range(C):
                        dma("Y", f"qV{r}",
                            sV[:, r * NLT * dh:(r + 1) * NLT * dh]
                            .rearrange("p (s2 d) -> p s2 d", d=dh),
                            gv_out[r * H * Tc + h * Tc:
                                   r * H * Tc + (h + 1) * Tc, :]
                            .rearrange("(s2 p) d -> p s2 d", p=128),
                            reads=[("gv_out",), ("V", r)], writes=[("V", r)])
                    for r in range(C):
                        dma("Y", f"qR{r}", sKr[0:dr // 2, r * Tc:(r + 1) * Tc],
                            gkr[b1:b1 + dr // 2, r, :],
                            reads=[("gk_out",), ("Kr", r)], writes=[("Kr", r)])
                        dma("Y", f"qR{r}", sKr[dr // 2:dr, r * Tc:(r + 1) * Tc],
                            gkr[b2:b2 + dr // 2, r, :],
                            reads=[("gk_out",)], writes=[("Kr", r)])
                    for qt in range(NQT):
                        q0 = qt * QTS
                        abk = "ps2"
                        NSC = KC // 2
                        assert NSC >= 5

                        def s_chain(ck, hh=h, q=q0):
                            sb = (ck // 2) % 2
                            lo = (ck % 2) * QTS
                            E("T", [
                                (lambda e, c=ck, b=sb, l=lo:
                                 e.matmul(psS[b][:, l:l + QTS],
                                          sKc[:, c * 128:(c + 1) * 128],
                                          sQcT[:, hh * Tc + q: hh * Tc + q + QTS],
                                          start=True, stop=False)),
                                (lambda e, c=ck, b=sb, l=lo:
                                 e.matmul(psS[b][:, l:l + QTS],
                                          sKr[:, c * 128:(c + 1) * 128],
                                          sQrT[:, hh * Tc + q: hh * Tc + q + QTS],
                                          start=False, stop=True)),
                            ], reads=[("Kc", ck // (KC // C)),
                                      ("Kr", ck // (KC // C)),
                                      ("QcT",), ("QrT",)],
                               writes=[(f"psS{sb}",)])

                        def do_av(sc):
                            pb = sc % 6
                            for half in range(2):
                                ck = 2 * sc + half
                                av_writes = [("ps2",)] if ck in (0, KC - 1) else []
                                E("T", lambda e, c=ck, p=pb, l=half * QTS,
                                  fi=(ck == 0), la=(ck == KC - 1):
                                  e.matmul(psAV[:, 0:QTS],
                                           sV[:, c * 128:(c + 1) * 128],
                                           sP[p][:, l:l + QTS], start=fi, stop=la),
                                  reads=[("P", pb), ("V", ck // (KC // C))],
                                  writes=av_writes)

                        s_chain(0)
                        s_chain(1)
                        vfirst, gfirst = [True], [True]
                        for sc in range(NSC):
                            sb = sc % 2
                            pb = sc % 6
                            E("S", lambda e, b=sb, p=pb: e.activation(
                                out=sP[p][:, 0:2 * QTS], in_=psS[b][:, 0:2 * QTS],
                                func=AF.Exp, scale=scale),
                              reads=[(f"psS{sb}",), ("P", pb)], writes=[("P", pb)])
                            if sc + 1 < NSC:
                                s_chain(2 * sc + 2)
                                s_chain(2 * sc + 3)
                            if sc >= 1:
                                fin_len = 8 + QTS // 128
                                npop = -(-fin_len // (NSC - 1))
                                for _ in range(npop):
                                    if pending_fin:
                                        pending_fin.pop(0)()
                            if sc >= 2:
                                do_av(sc - 2)
                            if sc == NSC - 1:
                                do_av(sc - 1)
                                do_av(sc)
                            sel = ("V", "V", "G", "G", "V")[sc % 5]
                            eng = "V" if sel == "V" else "G"
                            tgt = ("racc_v",) if sel == "V" else ("racc_g",)
                            dstb = sracc_v if sel == "V" else sracc_g
                            fref = vfirst if sel == "V" else gfirst
                            if fref[0]:
                                fref[0] = False
                                E(eng, lambda e, p=pb, d=dstb: e.tensor_copy(
                                    d[:, 0:2 * QTS], sP[p][:, 0:2 * QTS]),
                                  reads=[("P", pb), tgt], writes=[tgt])
                            else:
                                E(eng, lambda e, p=pb, d=dstb: e.tensor_tensor(
                                    out=d[:, 0:2 * QTS], in0=d[:, 0:2 * QTS],
                                    in1=sP[p][:, 0:2 * QTS], op=AluOpType.add),
                                  reads=[("P", pb), tgt], writes=[tgt])
                            if mat_queue:
                                mat_queue.pop(0)()
                                if len(mat_queue) > 40 and mat_queue:
                                    mat_queue.pop(0)()
                        E("V", lambda e: e.tensor_tensor(
                            out=sracc2[:, 0:2 * QTS], in0=sracc_v[:, 0:2 * QTS],
                            in1=sracc_g[:, 0:2 * QTS], op=AluOpType.add),
                          reads=[("racc_v",), ("racc_g",), ("racc2",)],
                          writes=[("racc2",)])
                        E("V", lambda e: e.tensor_tensor(
                            out=sracc[:, 0:QTS], in0=sracc2[:, 0:QTS],
                            in1=sracc2[:, QTS:2 * QTS], op=AluOpType.add),
                          reads=[("racc2",), ("racc",)], writes=[("racc",)])
                        E("S", lambda e: e.copy(soT[:, 0:QTS], psAV[:, 0:QTS]),
                          reads=[("ps2",), ("soT",)], writes=[("soT",)])

                        def fin_steps(hh=h, qq=qt, q0f=q0):
                            nqs = QTS // 128
                            st = [
                                lambda: E("T", lambda e: e.matmul(
                                    psR[0:1, 0:QTS], sones[:, 0:1],
                                    sracc[:, 0:QTS], start=True, stop=True),
                                  reads=[("racc",), ("ones",)],
                                  writes=[("ps3",)]),
                                lambda: E("S", lambda e: e.copy(
                                    srow[0:1, 0:QTS], psR[0:1, 0:QTS]),
                                  reads=[("ps3",), ("srow",)],
                                  writes=[("srow",)]),
                                lambda: E("V", lambda e: e.reciprocal(
                                    srow[0:1, 0:QTS], srow[0:1, 0:QTS]),
                                  reads=[("srow",)], writes=[("srow",)]),
                                lambda: E("T", lambda e: e.matmul(
                                    psR[:, 0:QTS], sonesr[0:1, :],
                                    srow[0:1, 0:QTS], start=True, stop=True),
                                  reads=[("srow",), ("onesr",)],
                                  writes=[("ps3",)]),
                                lambda: E("S", lambda e: e.copy(
                                    srbc[:, 0:QTS], psR[:, 0:QTS]),
                                  reads=[("ps3",), ("srbc",)],
                                  writes=[("srbc",)]),
                                lambda: E("V", lambda e: e.scalar_tensor_tensor(
                                    out=soT2[:, 0:QTS], in0=srbc[:, 0:QTS],
                                    scalar=1.0, in1=soT[:, 0:QTS],
                                    op0=AluOpType.mult, op1=AluOpType.mult),
                                  reads=[("soT",), ("srbc",), ("soT2",)],
                                  writes=[("soT2",)]),
                                lambda: E("V", lambda e, hh=hh: e.tensor_scalar(
                                    out=soT2[:, 0:QTS], in0=soT2[:, 0:QTS],
                                    scalar1=sbuv[:, hh:hh + 1], scalar2=None,
                                    op0=AluOpType.add),
                                  reads=[("soT2",), ("buv",)],
                                  writes=[("soT2",)]),
                            ]
                            for i in range(nqs):
                                tp = i % 2
                                st.append(lambda ii=i, t=tp: (
                                    E("T", lambda e: e.transpose(
                                        pt[t][:, 0:128],
                                        soT2[:, ii * 128:(ii + 1) * 128],
                                        sident[:, :]),
                                      reads=[("soT2",), ("ident",)],
                                      writes=[(f"pt{t}",)]),
                                    E("V", lambda e: e.tensor_copy(
                                        sanst[:, ii * 128:(ii + 1) * 128],
                                        pt[t][:, 0:128]),
                                      reads=[(f"pt{t}",), ("anst",)],
                                      writes=[("anst",)])))
                            st.append(lambda: dma("Y", "qAT",
                                attn_hbm[q0f:q0f + QTS, hh * dh:(hh + 1) * dh]
                                .rearrange("(i p) d -> p i d", p=128),
                                sanst[:, 0:QTS].rearrange("p (i d) -> p i d",
                                                          d=128),
                                reads=[("anst",)],
                                writes=[("attn_done", hh, qq)]))
                            return st

                        assert not pending_fin
                        pending_fin.extend(fin_steps())
                    for _ in range(2):
                        if mat_next[0] < n_gu_jobs:
                            plan_mat_job(mat_next[0])
                            mat_next[0] += 1

                while pending_fin:
                    pending_fin.pop(0)()

                while mat_next[0] < n_gu_jobs:
                    plan_mat_job(mat_next[0])
                    mat_next[0] += 1
                while mat_queue:
                    mat_queue.pop(0)()
                wreads = []
                for j, mj in enumerate(mat_jobs):
                    if mj[0] == "gu":
                        wreads += [("gw", j, k) for k in range(mj[4])]
                E("G", lambda e: e.collective_compute(
                    "AllGather", AluOpType.bypass,
                    replica_groups=[list(range(C))],
                    ins=[gw_in.ap().opt()], outs=[gw_out.ap().opt()]),
                  reads=wreads, writes=[("gw_out",)], coll="ccW")
                while mat_next[0] < njobs:
                    plan_mat_job(mat_next[0])
                    mat_next[0] += 1
                while mat_queue:
                    mat_queue.pop(0)()
                dreads = [("gw", j, 0) for j, mj in enumerate(mat_jobs)
                          if mj[0] == "dn"]
                E("G", lambda e: e.collective_compute(
                    "AllGather", AluOpType.bypass,
                    replica_groups=[list(range(C))],
                    ins=[gd_in.ap().opt()], outs=[gd_out.ap().opt()]),
                  reads=dreads, writes=[("gd_out",)], coll="ccD")

        P.barrier(exclude=("ccD",))

        with ExitStack() as phC:
            ce = phC.enter_context
            swg = [ce(nc.sbuf_tensor(f"swg{i}", [128, 2 * DT * 128], BF16))
                   for i in range(2)]
            shT = ce(nc.sbuf_tensor("shT", [128, OT * Tc], BF16))
            sg = ce(nc.sbuf_tensor("sg", [128, 512], BF16))
            swd = [ce(nc.sbuf_tensor(f"swd{i}", [128, NSLOT * C * 128], BF16))
                   for i in range(2)]
            gw4 = gw_out.rearrange("(r t p) q -> p r t q", p=128, t=2 * WG_T)
            gd4 = gd_out.rearrange("(r t p) q -> p r t q", p=128, t=WD_T)
            sfT = ce(nc.sbuf_tensor("sfT", [128, 512], BF16))
            sfn = ce(nc.sbuf_tensor("sfn", [128, (STS // 128) * D], BF16))
            sxc = ce(nc.sbuf_tensor("sxc", [128, D], F32))
            satn = ce(nc.sbuf_tensor("satn", [128, D], BF16))

            nb = HSL // 128
            for ot in range(OT):
                par = ot % 2
                gtok = None
                for mi in range(2):
                    base = mi * WG_T + ot * nb
                    for r in range(C):
                        c0 = (mi * DT + r * nb) * 128
                        gtok = dma("Y", f"qFG{par}",
                                   swg[par][:, c0:c0 + nb * 128]
                                   .rearrange("p (j q) -> p j q", q=128),
                                   gw4[:, r, base:base + nb, :],
                                   reads=[("gw_out",)],
                                   writes=[("swg", par)] if (mi == 0 and r == 0) else [])
                P.set_tok(("swg", par), gtok)
                for st in range(NST):
                    gb = st % 2
                    ub = 2 + st % 2
                    E("T", [
                        (lambda e, s=st, cc=ccc, p=par, fi=(ccc == 0),
                         la=(ccc == DT - 1), b=gb:
                         e.matmul(ps[b][:, 0:STS],
                                  swg[p][:, cc * 128: cc * 128 + 128],
                                  sxnT[:, cc * Tc + s * STS: cc * Tc + s * STS + STS],
                                  start=fi, stop=la))
                        for ccc in range(DT)],
                      reads=[("swg", par)] + xnT_keys, writes=[(f"ps{gb}",)])
                    E("T", [
                        (lambda e, s=st, cc=ccc, p=par, fi=(ccc == 0),
                         la=(ccc == DT - 1), b=ub:
                         e.matmul(ps[b][:, 0:STS],
                                  swg[p][:, (DT + cc) * 128: (DT + cc) * 128 + 128],
                                  sxnT[:, cc * Tc + s * STS: cc * Tc + s * STS + STS],
                                  start=fi, stop=la))
                        for ccc in range(DT)],
                      reads=[("swg", par)] + xnT_keys, writes=[(f"ps{ub}",)])
                    E("S", lambda e, b=gb: e.activation(
                        out=sg[:, 0:STS], in_=ps[b][:, 0:STS], func=AF.Silu),
                      reads=[(f"ps{gb}",), ("sg",)], writes=[("sg",)])
                    E("V", lambda e, o=ot, s=st, b=ub: e.scalar_tensor_tensor(
                        out=shT[:, o * Tc + s * STS: o * Tc + s * STS + STS],
                        in0=ps[b][:, 0:STS], scalar=1.0, in1=sg[:, 0:STS],
                        op0=AluOpType.mult, op1=AluOpType.mult),
                      reads=[(f"ps{ub}",), ("sg",)], writes=[("hT", ot)])

            hT_keys = [("hT", o) for o in range(OT)]
            for st in range(NST):
                for dt in range(DT):
                    par = (st * DT + dt) % 2
                    gtok = None
                    for r in range(C):
                        gtok = dma("Y", f"qFD{par}",
                                   swd[par][:, r * 128:]
                                   .rearrange("p (s q) -> p s q", q=128)
                                   [:, 0:NSLOT * C:C, :] if False else
                                   swd[par][:, :]
                                   .rearrange("p (s r2 q) -> p s r2 q", r2=C, q=128)
                                   [:, :, r, :],
                                   gd4[:, r, dt * NSLOT:(dt + 1) * NSLOT, :],
                                   reads=[("gd_out",)],
                                   writes=[("swd", par)] if r == 0 else [])
                    P.set_tok(("swd", par), gtok)
                    db = dt % 2
                    E("T", [
                        (lambda e, s=st, o=ooo, p=par, fi=(ooo == 0),
                         la=(ooo == OT - 1), b=db:
                         e.matmul(ps[b][:, 0:STS],
                                  swd[p][:, o * 128:(o + 1) * 128],
                                  shT[:, o * Tc + s * STS: o * Tc + s * STS + STS],
                                  start=fi, stop=la))
                        for ooo in range(OT)],
                      reads=[("swd", par)] + hT_keys, writes=[(f"ps{db}",)])
                    E("S", lambda e, b=db: e.copy(sfT[:, 0:STS], ps[b][:, 0:STS]),
                      reads=[(f"ps{db}",), ("sfT",)], writes=[("sfT",)])
                    for i in range(STS // 128):
                        tp = i % 2
                        E("T", lambda e, ii=i, t=tp: e.transpose(
                            pt[t][:, 0:128], sfT[:, ii * 128:(ii + 1) * 128],
                            sident[:, :]),
                          reads=[("sfT",), ("ident",)], writes=[(f"pt{tp}",)])
                        E("V", lambda e, d=dt, ii=i, t=tp: e.tensor_copy(
                            sfn[:, ii * D + d * 128: ii * D + (d + 1) * 128],
                            pt[t][:, 0:128]),
                          reads=[(f"pt{tp}",)], writes=[("fn", i, dt)])
                for ii in range(STS // 128):
                    j = st * (STS // 128) + ii
                    dma("Y", "qCX", sxc[:, :], x_in[j * 128:(j + 1) * 128, :],
                        reads=[("sxc",)], writes=[("sxc",)])
                    dma("Y", "qCA", satn[:, :], attn_hbm[j * 128:(j + 1) * 128, :],
                        reads=[("satn",)], writes=[("satn",)])
                    E("V", lambda e: e.tensor_tensor(
                        out=sxc[:, :], in0=sxc[:, :], in1=satn[:, :],
                        op=AluOpType.add),
                      reads=[("sxc",), ("satn",)], writes=[("sxc",)])
                    E("V", lambda e, i2=ii: e.tensor_tensor(
                        out=sxc[:, :], in0=sxc[:, :],
                        in1=sfn[:, i2 * D:(i2 + 1) * D], op=AluOpType.add),
                      reads=[("sxc",)] + [("fn", ii, d) for d in range(DT)],
                      writes=[("sxc",)])
                    dma("Y", "qY", y_out[j * 128:(j + 1) * 128, :], sxc[:, :],
                        reads=[("sxc",)], writes=[("y", j)])

        P.barrier()

        block = oe(nc.Block())

        @block.tensor
        def _(t):
            P.run("T", t, sems)

        @block.vector
        def _(v):
            P.run("V", v, sems)

        @block.scalar
        def _(s):
            P.run("S", s, sems)

        @block.gpsimd
        def _(g):
            P.run("G", g, sems)

        @block.sync
        def _(y):
            P.run("Y", y, sems)

    return nc



def _host_tables(cfg, core):
    Tc, dr = cfg["Tc"], cfg["dr"]
    t = (np.arange(Tc, dtype=np.float32) + np.float32(core * Tc))
    freqs = (10000.0 ** (-(np.arange(0, dr, 2, dtype=np.float32) / dr))).astype(np.float32)
    theta = freqs[:, None] * t[None, :]
    cos = np.cos(theta).astype(np.float32)
    sin = np.sin(theta).astype(np.float32)
    reps = 128 // (dr // 2)
    cos16 = np.ascontiguousarray(np.tile(cos, (reps, 1)))
    sin16 = np.ascontiguousarray(np.tile(sin, (reps, 1)))
    sinpm = np.ascontiguousarray(np.concatenate([-sin, sin, -sin, sin], axis=0))
    return cos16, sin16, sinpm


def _perm_cols(W, H, dr, swapped):
    cols = []
    for h in range(H):
        base = h * dr
        ev = list(range(base, base + dr, 2))
        od = list(range(base + 1, base + dr, 2))
        cols += (od + ev) if swapped else (ev + od)
    return W[..., cols]


def _perm_cols_grouped(W, H, dr):
    hpg = 128 // (dr // 2)
    cols = []
    for g in range(H // hpg):
        for part in range(2):
            for hh in range(hpg):
                base = (g * hpg + hh) * dr
                cols += list(range(base + part, base + dr, 2))
    return W[..., cols]


def make_in_maps(cfg, inputs):
    C, Tc, H, dr, D, HSL, NSLOT, OT = (cfg["C"], cfg["Tc"], cfg["H"], cfg["dr"],
                                       cfg["D"], cfg["HSL"], cfg["NSLOT"], cfg["OT"])
    x = np.ascontiguousarray(np.asarray(inputs["L0_attn_x"], dtype=np.float32)
                             .reshape(cfg["T"], D))
    f32 = lambda k: np.ascontiguousarray(np.asarray(inputs[k], dtype=np.float32))
    W_QR, W_KR = f32("L0_attn_W_QR"), f32("L0_attn_W_KR")
    b_QR, b_KR = f32("L0_attn_b_QR"), f32("L0_attn_b_KR")
    gidx = np.asarray(inputs["L0_ffn_gate_idx"], dtype=np.int32)
    uidx = np.asarray(inputs["L0_ffn_up_idx"], dtype=np.int32)
    didx = np.asarray(inputs["L0_ffn_down_idx"], dtype=np.int32)
    ident = np.eye(128, dtype=BF)

    w_qr0 = np.ascontiguousarray(_perm_cols(W_QR, H, dr, False))
    w_qr1 = np.ascontiguousarray(_perm_cols(W_QR, H, dr, True))
    b_qr0 = np.ascontiguousarray(_perm_cols(b_QR[None, :], H, dr, False)[0])
    b_qr1 = np.ascontiguousarray(_perm_cols(b_QR[None, :], H, dr, True)[0])
    w_krg = np.ascontiguousarray(_perm_cols_grouped(W_KR, H, dr))
    b_krg = np.ascontiguousarray(_perm_cols_grouped(b_KR[None, :], H, dr)[0])

    in_maps = []
    for c in range(C):
        cos16, sin16, sinpm = _host_tables(cfg, c)
        slots = []
        for s in range(NSLOT):
            ot = c + C * s
            if ot < OT:
                slots.append(didx[:, ot * 128:(ot + 1) * 128])
            else:
                slots.append(np.zeros((D, 128), np.int32))
        didx_c = np.ascontiguousarray(np.concatenate(slots, axis=1))
        pc = lambda v, n: np.ascontiguousarray(v.reshape(n, 128).T)
        in_maps.append({
            "x": np.ascontiguousarray(x[c * Tc:(c + 1) * Tc]),
            "w_dkv": f32("L0_attn_W_DKV"), "w_uk": f32("L0_attn_W_UK"),
            "w_uv": f32("L0_attn_W_UV"), "w_dq": f32("L0_attn_W_DQ"),
            "w_uq": f32("L0_attn_W_UQ"),
            "w_qr0": w_qr0, "w_qr1": w_qr1, "w_krg": w_krg,
            "b_dkv": pc(f32("L0_attn_b_DKV"), cfg["DCT"]),
            "b_uk": pc(f32("L0_attn_b_UK"), cfg["DT"]),
            "b_uv": pc(f32("L0_attn_b_UV"), H),
            "b_dq": pc(f32("L0_attn_b_DQ"), cfg["DCT"]),
            "b_uq": pc(f32("L0_attn_b_UQ"), cfg["DT"]),
            "b_qr0": np.ascontiguousarray(b_qr0.reshape(H // 2, 2 * dr).T),
            "b_qr1": np.ascontiguousarray(b_qr1.reshape(H // 2, 2 * dr).T),
            "b_krg": pc(b_krg, cfg["DCT"]),
            "lut": f32("L0_ffn_lut"),
            "gidx": np.ascontiguousarray(gidx[:, c * HSL:(c + 1) * HSL]),
            "uidx": np.ascontiguousarray(uidx[:, c * HSL:(c + 1) * HSL]),
            "didx": didx_c,
            "cos16": cos16, "sin16": sin16, "sinpm": sinpm,
            "ident": ident,
        })
    return in_maps


def kernel(**inputs):
    cfg = make_cfg(mini=False)
    in_maps = make_in_maps(cfg, inputs)
    nc = build_program(cfg)
    res = run_bass_kernel_spmd(nc, in_maps, core_ids=list(range(cfg["C"])))
    out = np.concatenate([res.results[c]["y"] for c in range(cfg["C"])], axis=0)
    return out.reshape(1, cfg["T"], cfg["D"]).astype(np.float32)
